# version 17
# baseline (speedup 1.0000x reference)
import numpy as np
import ml_dtypes

import concourse.bass as bass
import concourse.mybir as mybir
from concourse.bass_utils import run_bass_kernel_spmd

F32 = mybir.dt.float32
BF16 = mybir.dt.bfloat16
I8 = mybir.dt.int8
OP = mybir.AluOpType
AF = mybir.ActivationFunctionType
AX = mybir.AxisListType

NCORES = 8
H, DK, DV, D = 4, 256, 256, 1024
GT = 128
NSEQ = 4
NFREQ = 81
NDFT = 160

BF = np.dtype(ml_dtypes.bfloat16)


class Res:
    __slots__ = ("name", "writer", "readers")

    def __init__(self, name):
        self.name = name
        self.writer = None
        self.readers = []


class Plan:

    ENGINES = ("sync", "tensor", "vector", "scalar", "gpsimd")

    def __init__(self):
        self.ops = {e: [] for e in self.ENGINES}
        self.n = {e: 0 for e in self.ENGINES}
        self.barrier = set()

    def emit(self, eng, fn, reads=(), writes=(), dma=False, sem=None):
        idx = self.n[eng]
        handle = (eng, idx)
        waits = set()
        st_set = getattr(self, "strict", False)
        if st_set is True:
            st_set = set(self.ENGINES)
        elif st_set is False:
            st_set = set()
        strict = eng in st_set
        for r in reads:
            if r.writer is not None and (strict or r.writer[0] != eng):
                waits.add(r.writer)
        for r in writes:
            if r.writer is not None and (strict or r.writer[0] != eng):
                waits.add(r.writer)
            for rd in r.readers:
                if strict or rd[0] != eng:
                    waits.add(rd)
        for r in reads:
            r.readers.append(handle)
        for r in writes:
            r.writer = handle
            r.readers = []
        if dma and sem is None:
            raise RuntimeError("DMA ops must name a semaphore chain")
        self.ops[eng].append([fn, waits, False, dma, sem or eng])
        self.n[eng] = idx + 1
        return handle

    def finalize(self):
        for eng in self.ENGINES:
            for op in self.ops[eng]:
                if op[3]:
                    op[2] = True
                for (peng, pidx) in op[1]:
                    self.ops[peng][pidx][2] = True
        cum = {}
        tot = {}
        per_sem = {}
        for eng in self.ENGINES:
            for i, op in enumerate(self.ops[eng]):
                if op[2]:
                    k = op[4]
                    per_sem[k] = per_sem.get(k, 0) + (16 if op[3] else 1)
                    cum[(eng, i)] = (k, per_sem[k])
        tot.update(per_sem)
        self.cum, self.tot = cum, tot
        return cum

    def replay(self, eng_name, eng, sems, cum):
        seen = {}
        for i, (fn, waits, marked, dma, semkey) in enumerate(self.ops[eng_name]):
            need = {}
            for h in waits:
                if h not in self.cum:
                    continue
                k, v = self.cum[h]
                if k in self.barrier:
                    v = self.tot[k]
                if v > need.get(k, -1):
                    need[k] = v
            for k in sorted(need):
                if seen.get(k, -1) >= need[k]:
                    continue
                seen[k] = need[k]
                eng.wait_ge(sems[k], need[k])
            inst = fn(eng)
            if marked:
                if inst is None:
                    raise RuntimeError(f"marked op returned None on {eng_name}")
                inst.then_inc(sems[semkey], 16 if dma else 1)


def _host_consts():
    c = {}
    c["id_f32"] = np.eye(128, dtype=np.float32)
    c["id_bf"] = np.eye(128, dtype=BF)
    idx = np.arange(128)
    blk = idx // 32
    sameblk = blk[:, None] == blk[None, :]
    c["mask_A"] = np.where(idx[:, None] > idx[None, :], 1.0, 0.0).astype(BF)
    c["trif"] = (idx[:, None] <= idx[None, :]).astype(np.float32).astype(BF)
    i32 = np.arange(32)
    c["tri32"] = np.tile((i32[:, None] <= i32[None, :]).astype(np.float32), (4, 1)).astype(BF)
    c["tri32x4"] = np.tile((i32[:, None] <= i32[None, :]).astype(np.float32), (4, 4)).astype(BF)
    selpad = np.zeros((128, 65), np.float32)
    selpad[:, 32] = 1.0
    c["selpad"] = selpad.astype(BF)
    c["ones_row"] = np.ones((1, 128), np.float32)
    t_all = np.arange(NDFT)
    f = np.arange(NFREQ)
    ang = 2.0 * np.pi * np.outer(t_all, f) / NDFT
    Dr, Di = np.cos(ang), -np.sin(ang)
    c["dft_cur_r"] = Dr[30:158].astype(BF)
    c["dft_cur_i"] = Di[30:158].astype(BF)
    pr = np.zeros((32, NFREQ)); pr[:30] = Dr[0:30]
    pi_ = np.zeros((32, NFREQ)); pi_[:30] = Di[0:30]
    c["dft_prev_r"] = np.tile(pr, (4, 1)).astype(BF)
    c["dft_prev_i"] = np.tile(pi_, (4, 1)).astype(BF)
    cf = np.full(NFREQ, 2.0); cf[0] = 1.0; cf[80] = 1.0
    tout = np.arange(30, 158)
    ango = 2.0 * np.pi * np.outer(f, tout) / NDFT
    c["idft_r"] = ((cf[:, None] / NDFT) * np.cos(ango)).astype(BF)
    c["idft_i"] = (-(cf[:, None] / NDFT) * np.sin(ango)).astype(BF)
    K = 31
    lag = (K - 1) - np.arange(K)
    angw = 2.0 * np.pi * np.outer(lag, f) / NDFT
    er = np.zeros((32, NFREQ)); er[:31] = np.cos(angw)
    ei = np.zeros((32, NFREQ)); ei[:31] = -np.sin(angw)
    c["er"] = er.astype(BF)
    c["ei"] = ei.astype(BF)
    return c


def _prep_weights(inputs):
    w = {}

    def tile64(W):
        return np.ascontiguousarray(
            W.reshape(8, 128, 8, 128).transpose(2, 0, 1, 3).reshape(64, 128, 128))

    w["wq_t"] = tile64(np.asarray(inputs["L0_attn_W_q"], np.float32))
    w["wk_t"] = tile64(np.asarray(inputs["L0_attn_W_k"], np.float32))
    w["wv_t"] = tile64(np.asarray(inputs["L0_attn_W_v"], np.float32))
    W1 = np.asarray(inputs["L0_attn_W_fuse1"], np.float32)
    w["w1_t"] = np.ascontiguousarray(
        W1.reshape(32, 128, 4, 512).transpose(2, 0, 1, 3).reshape(128, 128, 512))
    Wo = np.asarray(inputs["L0_attn_W_o"], np.float32)
    w["wo_t"] = np.ascontiguousarray(
        Wo.reshape(8, 128, 2, 512).transpose(2, 0, 1, 3).reshape(16, 128, 512))
    Wf = np.asarray(inputs["L0_ffn_weight"], np.float32)
    WfT = np.ascontiguousarray(Wf.T)
    w["wft_t"] = np.ascontiguousarray(
        WfT.reshape(8, 128, 2, 512).transpose(2, 0, 1, 3).reshape(16, 128, 512))
    Wb = np.asarray(inputs["L0_attn_W_b"], np.float32)
    wbp = np.zeros((1024, 32), np.float32)
    wbp[:, 8:12] = Wb
    w["wb_t"] = np.ascontiguousarray(wbp.reshape(8, 128, 32))
    W2 = np.asarray(inputs["L0_attn_W_fuse2"], np.float32)
    perm = np.array([h * 4 + g for g in range(4) for h in range(4)])
    w["w2_t"] = np.ascontiguousarray(W2[:, perm].reshape(16, 128, 16))
    b2 = np.asarray(inputs["L0_attn_b_fuse2"], np.float32)
    w["b2_p"] = np.ascontiguousarray(b2[perm].reshape(16, 1))
    b1 = np.asarray(inputs["L0_attn_b_fuse1"], np.float32)
    w["b1_c"] = np.ascontiguousarray(b1.reshape(16, 128).T)
    w["lt12"] = np.ascontiguousarray(
        np.tile(np.asarray(inputs["L0_attn_others_log_tau"], np.float32), 3).reshape(12, 1))
    for nm, key in (("cw_q", "L0_attn_conv_w_q"), ("cw_k", "L0_attn_conv_w_k"),
                    ("cw_v", "L0_attn_conv_w_v")):
        cw = np.asarray(inputs[key], np.float32)
        w[nm] = np.ascontiguousarray(cw.reshape(8, 128, 4).transpose(1, 0, 2))
    fs = np.asarray(inputs["L0_attn_fir_short_w"], np.float32).reshape(1024, 3)
    w["fir_s"] = np.ascontiguousarray(fs.reshape(8, 128, 3).transpose(1, 0, 2))
    fl = np.asarray(inputs["L0_attn_fir_long_w"], np.float32).reshape(1024, 31)
    flT = np.zeros((32, 1024), np.float32)
    flT[:31] = fl.T
    w["firl_T"] = flT
    w["nw_t"] = np.ascontiguousarray(
        np.tile(np.asarray(inputs["L0_attn_norm_w"], np.float32), 4).reshape(8, 128).T)
    return w


PROBE = False
STRICT = {"vector"}
DETECT = False


def build_program(L=2048):
    NG = L // GT
    nc = bass.Bass("TRN2", target_bir_lowering=False,
                   detect_race_conditions=DETECT)
    P = Plan()
    P.strict = STRICT

    dr = {}

    def dt_in(name, shape, dt):
        dr[name] = nc.dram_tensor(name, list(shape), dt, kind="ExternalInput").ap()
        return dr[name]

    def dt_int(name, shape, dt):
        dr[name] = nc.dram_tensor(name, list(shape), dt, kind="Internal").ap()
        return dr[name]

    x_h = dt_in("x", (NSEQ, L, D), F32)
    out_h = nc.dram_tensor("out", [NSEQ, L, D], F32, kind="ExternalOutput").ap()

    for nm, shape, dt in [
        ("wq_t", (64, 128, 128), F32), ("wk_t", (64, 128, 128), F32),
        ("wv_t", (64, 128, 128), F32), ("w1_t", (128, 128, 512), F32),
        ("wo_t", (16, 128, 512), F32), ("wft_t", (16, 128, 512), F32),
        ("wb_t", (8, 128, 32), F32), ("w2_t", (16, 128, 16), F32),
        ("b2_p", (16, 1), F32), ("b1_c", (128, 16), F32), ("lt12", (12, 1), F32),
        ("cw_q", (128, 8, 4), F32), ("cw_k", (128, 8, 4), F32), ("cw_v", (128, 8, 4), F32),
        ("fir_s", (128, 8, 3), F32), ("firl_T", (32, 1024), F32), ("nw_t", (128, 8), F32),
        ("id_f32", (128, 128), F32), ("id_bf", (128, 128), BF16),
        ("mask_A", (128, 128), BF16), ("tri32", (128, 32), BF16),
        ("tri32x4", (128, 128), BF16), ("trif", (128, 128), BF16),
        ("selpad", (128, 65), BF16), ("ones_row", (1, 128), F32),
        ("dft_cur_r", (128, 81), BF16), ("dft_cur_i", (128, 81), BF16),
        ("dft_prev_r", (128, 81), BF16), ("dft_prev_i", (128, 81), BF16),
        ("idft_r", (81, 128), BF16), ("idft_i", (81, 128), BF16),
        ("er", (32, 81), BF16), ("ei", (32, 81), BF16),
    ]:
        dt_in(nm, shape, dt)

    if PROBE:
        def dt_out(name, shape, dt):
            dr[name] = nc.dram_tensor(name, list(shape), dt, kind="ExternalOutput").ap()
        dt_out("pr_dtm", (128, NSEQ, 1024), F32)
        dt_out("pr_stats", (128, NSEQ, 40), F32)
        dt_out("pr_otm", (NSEQ, 128, 1024), F32)
        dt_out("pr_orms", (NSEQ, 128, 1024), F32)
        dt_out("pr_slab", (32, 512), F32)
        dt_out("pr_xnfm", (128, 8, NSEQ, 128), BF16)
        dt_out("pr_qfm", (128, 8, NSEQ, 128), BF16)
        dt_out("pr_kfm", (128, 8, NSEQ, 128), BF16)
        dt_out("pr_vfm", (128, 8, NSEQ, 160), BF16)
        dt_out("pr_G", (128, NSEQ, 256), BF16)
        dt_out("pr_P", (128, NSEQ, 128), BF16)
        dt_out("pr_U", (128, NSEQ, 256), BF16)
        dt_out("pr_flfm", (128, 8, NSEQ, 128), BF16)
        dt_out("pr_fsfm", (128, 8, NSEQ, 128), BF16)
        dt_out("pr_A", (128, NSEQ, 128), BF16)
        dt_out("pr_B", (128, NSEQ, 128), BF16)
        dt_out("pr_P1", (128, NSEQ, 128), BF16)
        dt_out("pr_A2", (4, 128, NSEQ, 128), BF16)
        dt_out("pr_Pl", (4, 128, NSEQ, 128), BF16)
    dt_int("wq_b", (64, 128, 128), BF16)
    dt_int("wk_b", (64, 128, 128), BF16)
    dt_int("wv_b", (64, 128, 128), BF16)
    dt_int("w1_b", (128, 128, 512), BF16)
    dt_int("wo_b", (16, 128, 512), BF16)
    dt_int("wft_b", (16, 128, 512), BF16)

    import contextlib
    es = contextlib.ExitStack()
    sb = {}

    def S(name, shape, dt):
        sb[name] = es.enter_context(nc.sbuf_tensor(name, list(shape), dt))
        return sb[name]

    with es:
        S("c_idf", (128, 128), F32); S("c_idb", (128, 128), BF16)
        S("c_mA", (128, 128), BF16); S("c_tri", (128, 32), BF16)
        S("c_tri4", (128, 128), BF16); S("c_trif", (128, 128), BF16)
        S("c_sel", (128, 65), BF16); S("c_or", (1, 128), F32)
        S("c_dcr", (128, 81), BF16); S("c_dci", (128, 81), BF16)
        S("c_dpr", (128, 81), BF16); S("c_dpi", (128, 81), BF16)
        S("c_ir", (81, 128), BF16); S("c_ii", (81, 128), BF16)
        S("c_er", (32, 81), BF16); S("c_ei", (32, 81), BF16)
        S("w_wb", (128, 8, 32), BF16)
        S("w_w2", (128, 16, 16), BF16)
        S("w_b1", (128, 16), F32); S("w_b2", (16, 1), F32)
        S("w_cwq", (128, 8, 4), F32); S("w_cwk", (128, 8, 4), F32)
        S("w_cwv", (128, 8, 4), F32); S("w_fs", (128, 8, 3), F32)
        S("w_nw", (128, 8), F32)
        S("taurec", (12, 1), F32)
        S("epsA", (128, 1), F32); S("epsB", (128, 1), F32); S("epsC", (128, 1), F32)
        S("whr", (128, 1024), BF16); S("whi", (128, 1024), BF16)
        S("ws127", (128, 1), F32); S("rws", (128, 1), F32)
        S("wcol", (128, 2), F32)
        S("rg_w", (128, 2, 1024), BF16)
        S("rg_w1", (128, 2, 2048), BF16)
        S("rg_wo", (128, 2, 2048), BF16)
        S("xbuf", (128, 2, 1024), F32)
        S("lnc", (128, 16), F32)
        S("sffn", (128, 4), F32)
        S("xn_tm", (128, 1024), F32)
        S("sc_a", (128, 1024), F32)
        S("aq8", (128, 1024), I8)
        S("aq_tb", (128, 1024), BF16)
        S("slab2", (8, 512), F32)
        S("xn_fm", (128, 8, NSEQ, 128), BF16)
        S("aq_fm", (128, 8, NSEQ, 128), BF16)
        S("q_fm", (128, 8, NSEQ, 128), BF16)
        S("k_fm", (128, 8, NSEQ, 128), BF16)
        S("v_fm", (128, 8, NSEQ, 160), BF16)
        S("prebuf", (128, 2, NSEQ, 132), BF16)
        S("phal", (128, 3, 8, NSEQ, 4), BF16)
        S("cacc", (128, 2, NSEQ, 128), BF16)
        S("q2b", (128, 2, 512), BF16)
        S("k_tm", (128, NSEQ, 1024), BF16)
        S("v_tm", (128, NSEQ, 1024), BF16)
        S("v_tl", (128, 1024), BF16)
        S("d_tm", (128, NSEQ, 1024), BF16)
        S("slab", (32, 512), F32)
        S("stats", (128, NSEQ, 40), F32)
        S("G_sb", (128, NSEQ, 256), BF16)
        S("A_sb", (128, NSEQ, 128), BF16)
        S("B_sb", (128, NSEQ, 128), BF16)
        S("P_sb", (128, NSEQ, 128), BF16)
        S("loc_sb", (128, NSEQ, 32), BF16)
        S("ybuf", (128, NSEQ, 256), BF16)
        S("vb_sb", (128, NSEQ, 256), BF16)
        S("U_pk", (128, NSEQ, 256), BF16)
        S("S32", (128, 16, 512), F32)
        S("S16", (128, 16, 512), BF16)
        S("hidr", (128, 2, 512), BF16)
        S("xh", (128, 2, 512), BF16)
        S("yh", (128, 4, 512), BF16)
        S("o2c", (128, 4), F32)
        S("gt4", (128, 8), F32)
        S("outb", (128, 2, 512), F32)

        pb = []
        for i in range(8):
            if i in (2, 3):
                pb.append(es.enter_context(nc.psum_tensor(f"pb{i}", [128, 1024], BF16)))
            else:
                pb.append(es.enter_context(nc.psum_tensor(f"pb{i}", [128, 512], F32)))

        sb["fs_fm"] = sb["q_fm"]
        sb["o_tm"] = sb["U_pk"]
        sb["fstm"] = sb["ybuf"]
        sb["fl_fm"] = sb["k_fm"]
        S("fl_tm", (128, NSEQ, 1024), BF16)
        sb["orms"] = sb["aq_tb"]

        sems = {}
        SEM_KEYS = list(Plan.ENGINES) + [
            "d_cst", "d_stg", "d_pre0", "d_pre1", "d_xb0", "d_xb1",
            "d_rgw0", "d_rgw1", "d_rgw2", "d_rg10", "d_rg11", "d_rg12",
            "d_rgo0", "d_rgo1", "d_out0", "d_out1", "d_vtl", "d_prb"]
        for e in SEM_KEYS:
            sems[e] = es.enter_context(nc.semaphore(f"sm_{e}"))
        P.barrier.add("d_cst")
        P.barrier.add("d_stg")
        P.barrier.add("d_prb")
        block = es.enter_context(nc.Block())

        R = {}

        def res(name):
            if name not in R:
                R[name] = Res(name)
            return R[name]

        PR = [res(f"psum{i}") for i in range(8)]
        E = P.emit

        def dma(eng, dst, src, sem, reads=(), writes=()):
            return E(eng, lambda q, d=dst, s_=src: q.dma_start(out=d, in_=s_),
                     reads=reads, writes=writes, dma=True, sem=sem)

        for dstn, srcn in [
            ("c_idf", "id_f32"), ("c_idb", "id_bf"), ("c_mA", "mask_A"),
            ("c_tri", "tri32"), ("c_tri4", "tri32x4"), ("c_trif", "trif"),
            ("c_sel", "selpad"), ("c_or", "ones_row"),
            ("c_dcr", "dft_cur_r"), ("c_dci", "dft_cur_i"),
            ("c_dpr", "dft_prev_r"), ("c_dpi", "dft_prev_i"),
            ("c_ir", "idft_r"), ("c_ii", "idft_i"), ("c_er", "er"), ("c_ei", "ei"),
            ("w_b1", "b1_c"), ("w_b2", "b2_p"),
            ("w_cwq", "cw_q"), ("w_cwk", "cw_k"), ("w_cwv", "cw_v"),
            ("w_fs", "fir_s"), ("w_nw", "nw_t"),
        ]:
            dma("sync", sb[dstn][:], dr[srcn], "d_cst", writes=[res(dstn)])
        dma("sync", sb["lnc"][0:12, 0:1], dr["lt12"], "d_cst", writes=[res("lnc")])

        E("vector", lambda v: v.memset(sb["epsA"][:], 1e-5), writes=[res("epsA")])
        E("vector", lambda v: v.memset(sb["epsB"][:], 1e-6), writes=[res("epsB")])
        E("vector", lambda v: v.memset(sb["epsC"][:], 1e-8), writes=[res("epsC")])
        for c8 in range(8):
            E("vector", lambda v, c8=c8: v.memset(sb["v_fm"][:, c8, :, 0:32], 0.0),
              writes=[res("v_fm")])
        E("vector", lambda v: v.memset(
            sb["phal"][:].rearrange("p a b c d -> p (a b c d)"), 0.0),
          writes=[res("phal")])
        E("vector", lambda v: v.memset(sb["v_tl"][:], 0.0), writes=[res("v_tl")])
        E("vector", lambda v: v.memset(sb["S32"][:].rearrange("p a b -> p (a b)"), 0.0),
          writes=[res("S32")])
        E("vector", lambda v: v.memset(sb["S16"][:].rearrange("p a b -> p (a b)"), 0.0),
          writes=[res("S16")])
        E("vector", lambda v: v.memset(sb["slab"][:], 0.0), writes=[res("slab")])
        E("vector", lambda v: v.memset(sb["slab2"][:], 0.0), writes=[res("slab2")])
        E("vector", lambda v: v.memset(sb["U_pk"][:].rearrange("p a b -> p (a b)"), 0.0),
          writes=[res("U_pk"), res("upk0"), res("upk1"), res("upk2"), res("upk3")])
        E("vector", lambda v: v.memset(sb["ybuf"][:].rearrange("p a b -> p (a b)"), 0.0),
          writes=[res("ybuf")])

        E("scalar", lambda s_: s_.activation(sb["taurec"][:], sb["lnc"][0:12, 0:1],
                                             AF.Exp, scale=-1.0),
          reads=[res("lnc")], writes=[res("taurec")])

        dma("sync", sb["sc_a"][:, 0:256].rearrange("p (c f) -> p c f", c=16),
            dr["w2_t"].rearrange("c p f -> p c f"), "d_stg", writes=[res("sc_a")])
        E("vector", lambda v: v.tensor_copy(
            sb["w_w2"][:].rearrange("p c f -> p (c f)"), sb["sc_a"][:, 0:256]),
          reads=[res("sc_a")], writes=[res("w_w2")])
        dma("sync", sb["xn_tm"][:, 0:256].rearrange("p (c f) -> p c f", c=8),
            dr["wb_t"].rearrange("c p f -> p c f"), "d_stg", writes=[res("xn_tm")])
        E("vector", lambda v: v.tensor_copy(
            sb["w_wb"][:].rearrange("p c f -> p (c f)"), sb["xn_tm"][:, 0:256]),
          reads=[res("xn_tm")], writes=[res("w_wb")])

        fl_stage = sb["outb"][:].rearrange("p a b -> p (a b)")
        dma("sync", fl_stage[0:32, :], dr["firl_T"], "d_stg",
            writes=[res("outb0"), res("outb1")])
        E("vector", lambda v, fl_stage=fl_stage: v.tensor_copy(
            sb["d_tm"][0:32, 1, :], fl_stage[0:32, :]),
          reads=[res("outb0")], writes=[res("d_tm")])
        for half in range(2):
            cs = slice(512 * half, 512 * half + 512)
            E("tensor", lambda t, cs=cs: t.matmul(
                pb[0][0:81, :], sb["c_er"][:], sb["d_tm"][0:32, 1, cs],
                start=True, stop=True, skip_group_check=True),
              reads=[res("d_tm"), res("c_er")], writes=[PR[0]])
            E("tensor", lambda t, cs=cs: t.matmul(
                pb[1][0:81, :], sb["c_ei"][:], sb["d_tm"][0:32, 1, cs],
                start=True, stop=True, skip_group_check=True),
              reads=[res("d_tm"), res("c_ei")], writes=[PR[1]])
            E("vector", lambda v, cs=cs: v.tensor_copy(sb["whr"][0:81, cs], pb[0][0:81, :]),
              reads=[PR[0]], writes=[res("whr")])
            E("vector", lambda v, cs=cs: v.tensor_copy(sb["whi"][0:81, cs], pb[1][0:81, :]),
              reads=[PR[1]], writes=[res("whi")])

        E("vector", lambda v: v.memset(sb["wcol"][:], 0.0), writes=[res("wcol")])
        for i in range(16):
            slot = i % 2
            dma("sync",
                sb["xbuf"][:, slot, 0:512].rearrange("p (a f) -> p a f", a=2),
                dr["wft_t"][i].rearrange("p (a f) -> p a f", a=2),
                f"d_xb{slot}", writes=[res(f"xbuf{slot}")])
            E("vector", lambda v, sl=slot: v.tensor_reduce(
                sb["wcol"][:, 1:2], sb["xbuf"][:, sl, 0:512], AX.X, OP.add,
                apply_absolute_value=True),
              reads=[res(f"xbuf{slot}")], writes=[res("wcol")])
            E("vector", lambda v: v.tensor_tensor(
                sb["wcol"][:, 0:1], sb["wcol"][:, 0:1], sb["wcol"][:, 1:2], OP.add),
              reads=[res("wcol")], writes=[res("wcol")])
        E("vector", lambda v: v.memset(sb["lnc"][:, 15:16], 1.0),
          reads=[res("taurec")], writes=[res("lnc")])
        E("tensor", lambda t: t.matmul(pb[7][0:1, 0:2], sb["lnc"][:, 15:16],
                                       sb["wcol"][:, 0:2], start=True, stop=True,
                                       skip_group_check=True),
          reads=[res("wcol"), res("lnc")], writes=[PR[7]])
        E("vector", lambda v: v.tensor_scalar(sb["wcol"][0:1, 0:1], pb[7][0:1, 0:1],
                                              1.0 / (1024.0 * 1024.0), 1e-8,
                                              OP.mult, OP.add),
          reads=[PR[7]], writes=[res("wcol")])
        E("tensor", lambda t: t.matmul(pb[7][0:128, 2:3], sb["c_or"][:],
                                       sb["wcol"][0:1, 0:1], start=True, stop=True,
                                       skip_group_check=True),
          reads=[res("wcol"), res("c_or")], writes=[PR[7]])
        E("vector", lambda v: v.tensor_scalar(sb["ws127"][:], pb[7][0:128, 2:3],
                                              1.0 / 127.0, None, OP.mult),
          reads=[PR[7]], writes=[res("ws127")])
        E("vector", lambda v: v.reciprocal(sb["rws"][:], pb[7][0:128, 2:3]),
          reads=[PR[7]], writes=[res("rws")])

        def cast_stream(src, dst, n, b, op=None, hres=None):
            for i in range(n):
                slot = i % 2
                stg = sb["d_tm"][:, 0, :] if slot == 0 else sb["aq_tb"]
                stg_res = res("d_tm" if slot == 0 else "aq_tb")
                dma("sync",
                    sb["xbuf"][:, slot, :].rearrange("p (b f) -> p b f", b=b),
                    src[i], f"d_xb{slot}", writes=[res(f"xbuf{slot}")])
                if op is None:
                    E("vector", lambda v, sl=slot, stg=stg: v.tensor_copy(
                        stg[:], sb["xbuf"][:, sl, :]),
                      reads=[res(f"xbuf{slot}")], writes=[stg_res])
                else:
                    op(i, slot, stg, stg_res)
                dma("gpsimd", dst[i],
                    stg[:].rearrange("p (b f) -> p b f", b=b),
                    f"d_pre{slot}", reads=[stg_res],
                    writes=[res(f"cdone{slot}")] + ([hres] if hres is not None else []))

        for srcn, dstn in (("wq_t", "wq_b"), ("wk_t", "wk_b"), ("wv_t", "wv_b")):
            cast_stream(dr[srcn].rearrange("(a b) p f -> a p b f", b=8),
                        dr[dstn].rearrange("(a b) p f -> a p b f", b=8), 8, 8,
                        hres=res("hbm_" + dstn))
        cast_stream(dr["w1_t"].rearrange("(a b) p f -> a p b f", b=2),
                    dr["w1_b"].rearrange("(a b) p f -> a p b f", b=2), 64, 2,
                    hres=res("hbm_w1_b"))

        def wo_op(i, slot, stg, stg_res):
            for tix in range(2):
                kc = (i * 2 + tix) % 8

                def f(v, sl=slot, k=kc, tix=tix, stg=stg):
                    return v.tensor_scalar(stg[:, 512 * tix:512 * tix + 512],
                                           sb["xbuf"][:, sl, 512 * tix:512 * tix + 512],
                                           sb["w_nw"][:, k:k + 1], None, OP.mult)
                E("vector", f, reads=[res(f"xbuf{slot}"), res("w_nw")], writes=[stg_res])
        cast_stream(dr["wo_t"].rearrange("(a b) p f -> a p b f", b=2),
                    dr["wo_b"].rearrange("(a b) p f -> a p b f", b=2), 8, 2, op=wo_op,
                    hres=res("hbm_wo_b"))

        def wft_op(i, slot, stg, stg_res):
            E("vector", lambda v, sl=slot: v.tensor_scalar(
                sb["sc_a"][:], sb["xbuf"][:, sl, :], sb["rws"][:, 0:1], 1.0,
                OP.mult, OP.min),
              reads=[res(f"xbuf{slot}"), res("rws")], writes=[res("sc_a")])
            E("vector", lambda v: v.tensor_scalar(
                sb["sc_a"][:], sb["sc_a"][:], -1.0, None, OP.max),
              reads=[res("sc_a")], writes=[res("sc_a")])
            E("vector", lambda v: v.tensor_scalar(
                sb["aq8"][:], sb["sc_a"][:], 0.0, None, OP.add),
              reads=[res("sc_a")], writes=[res("aq8")])
            dbw = sb["outb"][:].rearrange("p a b -> p (a b)")
            E("vector", lambda v, dbw=dbw: v.scalar_tensor_tensor(
                dbw, sb["aq8"][:], -1.0, sb["sc_a"][:], OP.mult, OP.add),
              reads=[res("aq8"), res("sc_a")], writes=[res("outb0"), res("outb1")])
            E("vector", lambda v, stg=stg, dbw=dbw: v.scalar_tensor_tensor(
                stg[:], dbw, 0.5, sb["aq8"][:], OP.is_gt, OP.add),
              reads=[res("outb0"), res("aq8")], writes=[stg_res])
            E("vector", lambda v, stg=stg, dbw=dbw: v.scalar_tensor_tensor(
                stg[:], dbw, -0.5, stg[:], OP.is_ge, OP.add),
              reads=[res("outb0")], writes=[stg_res])
            E("vector", lambda v, stg=stg: v.tensor_scalar(
                stg[:], stg[:], -1.0, None, OP.add),
              reads=[], writes=[stg_res])
        cast_stream(dr["wft_t"].rearrange("(a b) p f -> a p b f", b=2),
                    dr["wft_b"].rearrange("(a b) p f -> a p b f", b=2), 8, 2, op=wft_op,
                    hres=res("hbm_wft_b"))

        wq_b8 = dr["wq_b"].rearrange("(a b) p f -> a p b f", b=8)
        wk_b8 = dr["wk_b"].rearrange("(a b) p f -> a p b f", b=8)
        wv_b8 = dr["wv_b"].rearrange("(a b) p f -> a p b f", b=8)
        w1_b4 = dr["w1_b"].rearrange("(a b) p f -> a p b f", b=4)
        wo_b4 = dr["wo_b"].rearrange("(a b) p f -> a p b f", b=4)
        wf_b4 = dr["wft_b"].rearrange("(a b) p f -> a p b f", b=4)

        xfm, afm = sb["xn_fm"][:], sb["aq_fm"][:]
        qfm, kfm, vfm = sb["q_fm"][:], sb["k_fm"][:], sb["v_fm"][:]
        dfm = sb["v_fm"][:][:, :, :, 32:160]
        stt = sb["stats"]

        ctr = {"w": 0, "w1": 0, "wo": 0}

        wsrcs = (wq_b8, wk_b8, wv_b8)
        wsrc_nm = ("wq_b", "wk_b", "wv_b")

        for g in range(NG):
            t0 = g * GT

            grp_w_base = ctr["w"]

            def emit_rgw(i):
                wslot = (grp_w_base + i) % 2
                wi, mc = i // 8, i % 8
                dma("sync",
                    sb["rg_w"][:, wslot, :].rearrange("p (b f) -> p b f", b=8),
                    wsrcs[wi][mc], f"d_rgw{wslot}",
                    reads=[res("hbm_" + wsrc_nm[wi])],
                    writes=[res(f"rgw{wslot}")])

            dma("sync", sb["xbuf"][:, 0, :], x_h[0, t0:t0 + GT, :],
                "d_xb0", writes=[res("xbuf0")])
            dma("sync", sb["xbuf"][:, 1, :], x_h[1, t0:t0 + GT, :],
                "d_xb1", writes=[res("xbuf1")])
            for pf in range(2):
                emit_rgw(pf)
            for b in (0, 1):
                slot = ctr["wo"] % 2
                ctr["wo"] += 1
                dma("sync",
                    sb["rg_wo"][:, slot, :].rearrange("p (b f) -> p b f", b=4),
                    wo_b4[b], f"d_rgo{slot}",
                    reads=[res("hbm_wo_b")], writes=[res(f"rgo{slot}")])

            for s in range(NSEQ):
                xslot = s % 2
                if s >= 2:
                    dma("sync", sb["xbuf"][:, xslot, :], x_h[s, t0:t0 + GT, :],
                        f"d_xb{xslot}", writes=[res(f"xbuf{xslot}")])
                xb = sb["xbuf"][:, xslot, :]
                ln = sb["lnc"]
                E("vector", lambda v, xb=xb: v.tensor_reduce(ln[:, 0:1], xb, AX.X, OP.add),
                  reads=[res(f"xbuf{xslot}")], writes=[res("lnc")])
                E("vector", lambda v, xb=xb: v.scalar_tensor_tensor(
                    sb["xn_tm"][:], xb, 1.0, xb, OP.mult, OP.mult, accum_out=ln[:, 1:2]),
                  reads=[res(f"xbuf{xslot}")], writes=[res("lnc"), res("xn_tm")])
                E("vector", lambda v: v.tensor_scalar(ln[:, 2:3], ln[:, 0:1],
                                                      1.0 / D, None, OP.mult),
                  reads=[res("lnc")], writes=[res("lnc")])
                E("vector", lambda v: v.tensor_scalar(ln[:, 3:4], ln[:, 2:3],
                                                      ln[:, 2:3], None, OP.mult),
                  reads=[res("lnc")], writes=[res("lnc")])
                E("vector", lambda v: v.scalar_tensor_tensor(
                    ln[:, 4:5], ln[:, 1:2], 1.0 / D, ln[:, 3:4], OP.mult, OP.subtract),
                  reads=[res("lnc")], writes=[res("lnc")])
                E("scalar", lambda s_: s_.activation(ln[:, 5:6], ln[:, 4:5], AF.Sqrt,
                                                     bias=sb["epsA"][:, 0:1]),
                  reads=[res("lnc"), res("epsA")], writes=[res("lnc")])
                E("vector", lambda v: v.reciprocal(ln[:, 6:7], ln[:, 5:6]),
                  reads=[res("lnc")], writes=[res("lnc")])
                E("vector", lambda v, xb=xb: v.tensor_scalar(
                    sb["xn_tm"][:], xb, ln[:, 2:3], ln[:, 6:7], OP.subtract, OP.mult),
                  reads=[res(f"xbuf{xslot}"), res("lnc")], writes=[res("xn_tm")])
                xn16 = sb["q2b"][:].rearrange("p a b -> p (a b)")
                E("vector", lambda v, xn16=xn16: v.tensor_copy(xn16, sb["xn_tm"][:]),
                  reads=[res("xn_tm")], writes=[res("q2b")])
                for c8 in range(8):
                    E("tensor", lambda t, c8=c8, xn16=xn16: t.transpose(
                        pb[2][:, 128 * c8:128 * c8 + 128],
                        xn16[:, 128 * c8:128 * c8 + 128], sb["c_idb"][:]),
                      reads=[res("q2b"), res("c_idb")], writes=[PR[2]])
                E("vector", lambda v, s=s: v.tensor_copy(
                    xfm[:, :, s, :], pb[2][:].rearrange("p (c t) -> p c t", c=8)),
                  reads=[PR[2]], writes=[res("xn_fm")])
                E("vector", lambda v: v.reduce_max(ln[:, 7:8], sb["xn_tm"][:], axis=AX.X,
                                                   apply_absolute_value=True),
                  reads=[res("xn_tm")], writes=[res("lnc")])
                E("vector", lambda v: v.tensor_tensor(ln[:, 8:9], ln[:, 7:8],
                                                      sb["epsC"][:, 0:1], OP.add),
                  reads=[res("lnc"), res("epsC")], writes=[res("lnc")])
                E("vector", lambda v: v.reciprocal(ln[:, 9:10], ln[:, 8:9]),
                  reads=[res("lnc")], writes=[res("lnc")])
                E("vector", lambda v: v.tensor_scalar(ln[:, 10:11], ln[:, 9:10],
                                                      127.0, None, OP.mult),
                  reads=[res("lnc")], writes=[res("lnc")])
                E("vector", lambda v, s=s: v.tensor_scalar(
                    sb["sffn"][:, s:s + 1], ln[:, 8:9], sb["ws127"][:, 0:1], None, OP.mult),
                  reads=[res("lnc"), res("ws127")], writes=[res("sffn")])
                E("vector", lambda v: v.tensor_scalar(
                    sb["sc_a"][:], sb["xn_tm"][:], ln[:, 10:11], 127.0, OP.mult, OP.min),
                  reads=[res("xn_tm"), res("lnc")], writes=[res("sc_a")])
                E("vector", lambda v: v.tensor_scalar(
                    sb["sc_a"][:], sb["sc_a"][:], -128.0, None, OP.max),
                  reads=[res("sc_a")], writes=[res("sc_a")])
                E("vector", lambda v: v.tensor_scalar(
                    sb["aq8"][:], sb["sc_a"][:], 0.0, None, OP.add),
                  reads=[res("sc_a")], writes=[res("aq8")])
                dbuf = sb["xbuf"][:, xslot, :]
                E("vector", lambda v, dbuf=dbuf: v.scalar_tensor_tensor(
                    dbuf, sb["aq8"][:], -1.0, sb["sc_a"][:], OP.mult, OP.add),
                  reads=[res("aq8"), res("sc_a")], writes=[res(f"xbuf{xslot}")])
                E("vector", lambda v, dbuf=dbuf: v.scalar_tensor_tensor(
                    sb["aq_tb"][:], dbuf, 0.5, sb["aq8"][:],
                    OP.is_gt, OP.add),
                  reads=[res(f"xbuf{xslot}"), res("aq8")], writes=[res("aq_tb")])
                E("vector", lambda v, dbuf=dbuf: v.scalar_tensor_tensor(
                    sb["aq_tb"][:], dbuf, -0.5, sb["aq_tb"][:],
                    OP.is_ge, OP.add),
                  reads=[res(f"xbuf{xslot}"), res("aq_tb")], writes=[res("aq_tb")])
                E("vector", lambda v: v.tensor_scalar(
                    sb["aq_tb"][:], sb["aq_tb"][:], -1.0, None, OP.add),
                  reads=[res("aq_tb")], writes=[res("aq_tb")])
                for c8 in range(8):
                    E("tensor", lambda t, c8=c8: t.transpose(
                        pb[3][:, 128 * c8:128 * c8 + 128],
                        sb["aq_tb"][:, 128 * c8:128 * c8 + 128], sb["c_idb"][:]),
                      reads=[res("aq_tb"), res("c_idb")], writes=[PR[3]])
                E("vector", lambda v, s=s: v.tensor_copy(
                    afm[:, :, s, :], pb[3][:].rearrange("p (c t) -> p c t", c=8)),
                  reads=[PR[3]], writes=[res("aq_fm")])

            first_stats = [True]

            def stats_mm(lhsT, rhs, reads):
                st_ = first_stats[0]
                first_stats[0] = False
                E("tensor", lambda t, l=lhsT, r=rhs, st=st_: t.matmul(
                    pb[7][0:32, :], l, r, start=st, stop=False, skip_group_check=True),
                  reads=reads, writes=[PR[7]])

            for kc in range(8):
                stats_mm(sb["w_wb"][:, kc, :],
                         xfm[:, kc, :, :].rearrange("p s t -> p (s t)"),
                         [res("w_wb"), res("xn_fm")])

            conv_w = {"q": sb["w_cwq"], "k": sb["w_cwk"], "v": sb["w_cwv"]}
            dst_fm = {"q": qfm, "k": kfm, "v": vfm}
            for wi, wname in enumerate(("q", "k", "v")):
                for mc in range(8):
                    li = wi * 8 + mc
                    wslot = (grp_w_base + li) % 2
                    ctr["w"] += 1
                    bk = (0, 1, 4, 5)[mc % 4]
                    for kc in range(8):
                        E("tensor", lambda t, sl=wslot, kc=kc, bk=bk: t.matmul(
                            pb[bk][:], sb["rg_w"][:, sl, 128 * kc:128 * kc + 128],
                            xfm[:, kc, :, :].rearrange("p s t -> p (s t)"),
                            start=(kc == 0), stop=(kc == 7), skip_group_check=True),
                          reads=[res(f"rgw{wslot}"), res("xn_fm")], writes=[PR[bk]])
                    if li + 2 < 24:
                        emit_rgw(li + 2)
                    pslot = mc % 2
                    pbf = sb["prebuf"][:, pslot, :, :]
                    pb_res = res(f"prebuf{pslot}")
                    E("vector", lambda v, wi=wi, mc=mc, pbf=pbf: v.tensor_copy(
                        pbf[:, :, 0:4], sb["phal"][:, wi, mc, :, :]),
                      reads=[res("phal")], writes=[pb_res])
                    E("scalar", lambda s_, bk=bk, pbf=pbf: s_.copy(
                        pbf[:, :, 4:132], pb[bk][:].rearrange("p (s t) -> p s t", s=NSEQ)),
                      reads=[PR[bk]], writes=[pb_res])
                    E("vector", lambda v, wi=wi, mc=mc, pbf=pbf: v.tensor_copy(
                        sb["phal"][:, wi, mc, :, :], pbf[:, :, 128:132]),
                      reads=[pb_res], writes=[res("phal")])
                    ca = sb["cacc"][:, pslot, :, :]
                    ca_res = res(f"cacc{pslot}")
                    cw = conv_w[wname]
                    E("vector", lambda v, pbf=pbf, ca=ca, cw=cw, mc=mc: v.tensor_scalar(
                        ca, pbf[:, :, 4:132], cw[:, mc, 3:4], None, OP.mult),
                      reads=[pb_res], writes=[ca_res])
                    for j in range(3):
                        sh = 3 - j
                        E("vector", lambda v, pbf=pbf, ca=ca, cw=cw, mc=mc, j=j, sh=sh:
                          v.scalar_tensor_tensor(
                              ca, pbf[:, :, 4 - sh:132 - sh], cw[:, mc, j:j + 1],
                              ca, OP.mult, OP.add),
                          reads=[pb_res, ca_res], writes=[ca_res])
                    if wname == "v":
                        dst = vfm[:, mc, :, 32:160]
                    else:
                        dst = dst_fm[wname][:, mc, :, :]
                    E("scalar", lambda s_, ca=ca, dst=dst: s_.activation(dst, ca, AF.Silu),
                      reads=[ca_res], writes=[res(wname + "_fm")])
                    if wname in ("q", "k"):
                        qs = mc % 2
                        srcv = dst_fm[wname][:, mc, :, :].rearrange("p s t -> p (s t)")
                        E("vector", lambda v, srcv=srcv, qs=qs: v.tensor_tensor(
                            sb["q2b"][:, qs, :], srcv, srcv, OP.mult),
                          reads=[res(wname + "_fm")], writes=[res(f"q2b{qs}")])
                        row = 4 * wi + mc // 2
                        stats_mm(sb["c_sel"][:, 32 - row:64 - row], sb["q2b"][:, qs, :],
                                 [res(f"q2b{qs}"), res("c_sel")])

            if PROBE and g == 0:
                E("gpsimd", lambda q: q.dma_start(out=dr["pr_xnfm"], in_=xfm),
                  reads=[res("xn_fm")], dma=True, sem="d_prb")
                E("gpsimd", lambda q: q.dma_start(out=dr["pr_qfm"], in_=qfm),
                  reads=[res("q_fm")], dma=True, sem="d_prb")
                E("gpsimd", lambda q: q.dma_start(out=dr["pr_kfm"], in_=kfm),
                  reads=[res("k_fm")], dma=True, sem="d_prb")
                E("gpsimd", lambda q: q.dma_start(out=dr["pr_vfm"], in_=vfm),
                  reads=[res("v_fm")], dma=True, sem="d_prb")

            E("scalar", lambda s_: s_.activation(sb["slab"][0:32, :], pb[7][0:32, :],
                                                 AF.Sigmoid),
              reads=[PR[7]], writes=[res("slab")])
            E("scalar", lambda s_: s_.activation(sb["slab2"][0:8, :], pb[7][0:8, :],
                                                 AF.Sqrt, bias=sb["epsB"][0:8, 0:1]),
              reads=[PR[7], res("epsB")], writes=[res("slab2")])
            E("vector", lambda v: v.reciprocal(sb["slab"][0:8, :], sb["slab2"][0:8, :]),
              reads=[res("slab2")], writes=[res("slab")])
            for s in range(NSEQ):
                E("tensor", lambda t, s=s: t.transpose(
                    pb[6][:, 0:32], sb["slab"][0:32, 128 * s:128 * s + 128],
                    sb["c_idf"][0:32, 0:32]),
                  reads=[res("slab"), res("c_idf")], writes=[PR[6]])
                E("vector", lambda v, s=s: v.tensor_copy(stt[:, s, 0:12], pb[6][:, 0:12]),
                  reads=[PR[6]], writes=[res("stats")])
                E("vector", lambda v, s=s: v.tensor_tensor(
                    stt[:, s, 12:16], stt[:, s, 8:12], stt[:, s, 4:8], OP.mult),
                  reads=[res("stats")], writes=[res("stats")])
                E("vector", lambda v, s=s: v.tensor_tensor(
                    stt[:, s, 16:20], stt[:, s, 12:16], stt[:, s, 4:8], OP.mult),
                  reads=[res("stats")], writes=[res("stats")])
                E("vector", lambda v, s=s: v.tensor_scalar(
                    stt[:, s, 16:20], stt[:, s, 16:20], -1.0, None, OP.mult),
                  reads=[res("stats")], writes=[res("stats")])

            for s in range(NSEQ):
                for bi_, (srcn, dstn) in enumerate((("k_fm", "k_tm"), ("v_fm", "v_tm"))):
                    bk = 2 + bi_
                    for c8 in range(8):
                        if dstn == "v_tm":
                            inp = vfm[:, c8, s, 32:160]
                        else:
                            inp = kfm[:, c8, s, :]
                        E("tensor", lambda t, inp=inp, c8=c8, bk=bk: t.transpose(
                            pb[bk][:, 128 * c8:128 * c8 + 128], inp, sb["c_idb"][:]),
                          reads=[res(srcn), res("c_idb")], writes=[PR[bk]])
                    E("vector", lambda v, s=s, dstn=dstn, bk=bk: v.tensor_copy(
                        sb[dstn][:, s, :], pb[bk][:]),
                      reads=[PR[bk]], writes=[res(dstn)])

            def dft_emit(gi):
                u, seg = gi // 3, gi % 3
                s_, chalf = u // 2, u % 2
                cs = slice(512 * chalf, 512 * chalf + 512)
                if seg == 0:
                    E("tensor", lambda t, s_=s_, cs=cs: t.matmul(
                        pb[7][0:81, :], sb["c_dcr"][:], sb["v_tm"][:, s_, cs],
                        start=True, stop=False, skip_group_check=True),
                      reads=[res("v_tm"), res("c_dcr")], writes=[PR[7]])
                    E("tensor", lambda t, s_=s_, cs=cs: t.matmul(
                        pb[7][0:81, :], sb["c_dpr"][32 * s_:32 * s_ + 32, :],
                        sb["v_tl"][32 * s_:32 * s_ + 32, cs],
                        start=False, stop=True, skip_group_check=True,
                        tile_position=(32 * s_, 0)),
                      reads=[res("v_tl"), res("c_dpr")], writes=[PR[7]])
                    E("scalar", lambda sc: sc.copy(sb["xh"][0:81, 0, :], pb[7][0:81, :]),
                      reads=[PR[7]], writes=[res("xh0")])
                elif seg == 1:
                    E("tensor", lambda t, s_=s_, cs=cs: t.matmul(
                        pb[7][0:81, :], sb["c_dci"][:], sb["v_tm"][:, s_, cs],
                        start=True, stop=False, skip_group_check=True),
                      reads=[res("v_tm"), res("c_dci")], writes=[PR[7]])
                    E("tensor", lambda t, s_=s_, cs=cs: t.matmul(
                        pb[7][0:81, :], sb["c_dpi"][32 * s_:32 * s_ + 32, :],
                        sb["v_tl"][32 * s_:32 * s_ + 32, cs],
                        start=False, stop=True, skip_group_check=True,
                        tile_position=(32 * s_, 0)),
                      reads=[res("v_tl"), res("c_dpi")], writes=[PR[7]])
                    E("scalar", lambda sc: sc.copy(sb["xh"][0:81, 1, :], pb[7][0:81, :]),
                      reads=[PR[7]], writes=[res("xh1")])
                    E("vector", lambda v, cs=cs: v.tensor_tensor(
                        sb["yh"][0:81, 0, :], sb["xh"][0:81, 0, :], sb["whr"][0:81, cs],
                        OP.mult),
                      reads=[res("xh0"), res("whr")], writes=[res("yh0")])
                    E("vector", lambda v, cs=cs: v.tensor_tensor(
                        sb["yh"][0:81, 2, :], sb["xh"][0:81, 1, :], sb["whi"][0:81, cs],
                        OP.mult),
                      reads=[res("xh1"), res("whi")], writes=[res("yh2")])
                    E("vector", lambda v: v.tensor_tensor(
                        sb["yh"][0:81, 0, :], sb["yh"][0:81, 0, :], sb["yh"][0:81, 2, :],
                        OP.subtract),
                      reads=[res("yh0"), res("yh2")], writes=[res("yh0")])
                    E("vector", lambda v, cs=cs: v.tensor_tensor(
                        sb["yh"][0:81, 1, :], sb["xh"][0:81, 0, :], sb["whi"][0:81, cs],
                        OP.mult),
                      reads=[res("xh0"), res("whi")], writes=[res("yh1")])
                    E("vector", lambda v, cs=cs: v.tensor_tensor(
                        sb["yh"][0:81, 3, :], sb["xh"][0:81, 1, :], sb["whr"][0:81, cs],
                        OP.mult),
                      reads=[res("xh1"), res("whr")], writes=[res("yh3")])
                    E("vector", lambda v: v.tensor_tensor(
                        sb["yh"][0:81, 1, :], sb["yh"][0:81, 1, :], sb["yh"][0:81, 3, :],
                        OP.add),
                      reads=[res("yh1"), res("yh3")], writes=[res("yh1")])
                else:
                    E("tensor", lambda t: t.matmul(
                        pb[7][:, :], sb["c_ir"][:], sb["yh"][0:81, 0, :],
                        start=True, stop=False, skip_group_check=True),
                      reads=[res("yh0"), res("c_ir")], writes=[PR[7]])
                    E("tensor", lambda t: t.matmul(
                        pb[7][:, :], sb["c_ii"][:], sb["yh"][0:81, 1, :],
                        start=False, stop=True, skip_group_check=True),
                      reads=[res("yh1"), res("c_ii")], writes=[PR[7]])
                    E("scalar", lambda sc, s_=s_, cs=cs: sc.copy(
                        sb["fl_tm"][:, s_, cs], pb[7][:]),
                      reads=[PR[7]], writes=[res("fl_tm")])

            grp_w1_base = ctr["w1"]

            def emit_rg1(i):
                slot = (grp_w1_base + i) % 2
                dma("sync",
                    sb["rg_w1"][:, slot, :].rearrange("p (b f) -> p b f", b=4),
                    w1_b4[i], f"d_rg1{slot}",
                    reads=[res("hbm_w1_b")], writes=[res(f"rg1{slot}")])

            for pf in range(2):
                emit_rg1(pf)

            for h in range(H):
                kcs = (2 * h, 2 * h + 1)
                pxof = [0, 1, 4, 5]
                for s in range(NSEQ):
                    bk = pxof[s]
                    for ki, kc in enumerate(kcs):
                        E("tensor", lambda t, kc=kc, s=s, ki=ki, bk=bk: t.matmul(
                            pb[bk][:, 0:128], kfm[:, kc, s, :], kfm[:, kc, s, :],
                            start=(ki == 0), stop=False, skip_group_check=True),
                          reads=[res("k_fm")], writes=[PR[bk]])
                        E("tensor", lambda t, kc=kc, s=s, ki=ki, bk=bk: t.matmul(
                            pb[bk][:, 128:256], kfm[:, kc, s, :], qfm[:, kc, s, :],
                            start=False, stop=(ki == 1), skip_group_check=True),
                          reads=[res("k_fm"), res("q_fm")], writes=[PR[bk]])
                    E("vector", lambda v, s=s, bk=bk: v.tensor_copy(
                        sb["G_sb"][:, s, :], pb[bk][:, 0:256]),
                      reads=[PR[bk]], writes=[res("G_sb")])
                    E("vector", lambda v, s=s, h=h, bk=bk: v.scalar_tensor_tensor(
                        sb["A_sb"][:, s, :], pb[bk][:, 0:128],
                        stt[:, s, 16 + h:17 + h], sb["c_mA"][:], OP.mult, OP.mult),
                      reads=[PR[bk], res("stats"), res("c_mA")], writes=[res("A_sb")])
                    E("tensor", lambda t, s=s: t.transpose(
                        pb[3][:, 0:128], sb["A_sb"][:, s, :], sb["c_idb"][:]),
                      reads=[res("A_sb"), res("c_idb")], writes=[PR[3]])
                    E("vector", lambda v, s=s: v.tensor_copy(
                        sb["B_sb"][:, s, :], pb[3][:, 0:128]),
                      reads=[PR[3]], writes=[res("B_sb")])
                    E("vector", lambda v, s=s: v.tensor_tensor(
                        sb["P_sb"][:, s, :], sb["B_sb"][:, s, :], sb["c_idb"][:], OP.add),
                      reads=[res("B_sb"), res("c_idb")], writes=[res("P_sb")])
                if PROBE and g == 0 and h == 0:
                    E("gpsimd", lambda q: q.dma_start(out=dr["pr_A"], in_=sb["A_sb"][:]),
                      reads=[res("A_sb")], dma=True, sem="d_prb")
                    E("gpsimd", lambda q: q.dma_start(out=dr["pr_B"], in_=sb["B_sb"][:]),
                      reads=[res("B_sb")], dma=True, sem="d_prb")
                    E("gpsimd", lambda q: q.dma_start(out=dr["pr_P1"], in_=sb["P_sb"][:]),
                      reads=[res("P_sb")], dma=True, sem="d_prb")
                for s in range(NSEQ):
                    bk = pxof[s]
                    E("vector", lambda v, s=s, h=h: v.tensor_scalar(
                        sb["vb_sb"][:, s, :], sb["v_tm"][:, s, 256 * h:256 * h + 256],
                        stt[:, s, 12 + h:13 + h], None, OP.mult),
                      reads=[res("v_tm"), res("stats")], writes=[res("vb_sb")])
                    for ki, kc in enumerate(kcs):
                        E("tensor", lambda t, kc=kc, s=s, ki=ki, h=h, bk=bk: t.matmul(
                            pb[bk][:, 0:256], kfm[:, kc, s, :],
                            sb["S16"][:, s * 4 + h, 256 * ki:256 * ki + 256],
                            start=(ki == 0), stop=False, skip_group_check=True),
                          reads=[res("k_fm"), res("S16")], writes=[PR[bk]])
                        E("tensor", lambda t, kc=kc, s=s, ki=ki, h=h, bk=bk: t.matmul(
                            pb[bk][:, 256:512], qfm[:, kc, s, :],
                            sb["S16"][:, s * 4 + h, 256 * ki:256 * ki + 256],
                            start=False, stop=False, skip_group_check=True),
                          reads=[res("q_fm"), res("S16")], writes=[PR[bk]])
                for lev in range(6):
                    for s in range(NSEQ):
                        E("tensor", lambda t, s=s: t.matmul(
                            pb[6][:, 128 * s:128 * s + 128],
                            sb["B_sb"][:, s, :], sb["A_sb"][:, s, :],
                            start=(s == 0), stop=(s == 3), skip_group_check=True),
                          reads=[res("A_sb"), res("B_sb")],
                          writes=[PR[6]] + ([res("pu0"), res("pu1")]
                                            if s == 0 else []))
                    E("vector", lambda v: v.tensor_copy(
                        sb["A_sb"][:].rearrange("p a b -> p (a b)"), pb[6][:]),
                      reads=[PR[6]], writes=[res("A_sb")])
                    for s in range(NSEQ):
                        E("tensor", lambda t, s=s: t.transpose(
                            pb[3][:, 128 * s:128 * s + 128],
                            sb["A_sb"][:, s, :], sb["c_idb"][:]),
                          reads=[res("A_sb"), res("c_idb")], writes=[PR[3]])
                    E("vector", lambda v: v.tensor_copy(
                        sb["B_sb"][:].rearrange("p a b -> p (a b)"),
                        pb[3][:, 0:512]),
                      reads=[PR[3]], writes=[res("B_sb")])
                    for s in range(NSEQ):
                        E("tensor", lambda t, s=s: t.matmul(
                            pb[6][:, 128 * s:128 * s + 128],
                            sb["A_sb"][:, s, :], sb["P_sb"][:, s, :],
                            start=(s == 0), stop=False, skip_group_check=True),
                          reads=[res("A_sb"), res("P_sb")], writes=[PR[6]])
                        E("tensor", lambda t, s=s: t.matmul(
                            pb[6][:, 128 * s:128 * s + 128],
                            sb["c_idb"][:], sb["P_sb"][:, s, :],
                            start=False, stop=(s == 3), skip_group_check=True),
                          reads=[res("c_idb"), res("P_sb")], writes=[PR[6]])
                    E("vector", lambda v: v.tensor_copy(
                        sb["P_sb"][:].rearrange("p a b -> p (a b)"), pb[6][:]),
                      reads=[PR[6]], writes=[res("P_sb")])
                    dft_emit(h * 6 + lev)
                    if PROBE and g == 0 and h == 0 and lev < 4:
                        E("gpsimd", lambda q, lev=lev: q.dma_start(
                            out=dr["pr_A2"][lev], in_=sb["A_sb"][:]),
                          reads=[res("A_sb")], dma=True, sem="d_prb")
                        E("gpsimd", lambda q, lev=lev: q.dma_start(
                            out=dr["pr_Pl"][lev], in_=sb["P_sb"][:]),
                          reads=[res("P_sb")], dma=True, sem="d_prb")
                for s in range(NSEQ):
                    bk = pxof[s]
                    ub, uc = ((6, 0), (6, 256), (7, 0), (7, 256))[s]
                    E("vector", lambda v, s=s, h=h, bk=bk: v.scalar_tensor_tensor(
                        sb["ybuf"][:, s, :], pb[bk][:, 0:256],
                        stt[:, s, 16 + h:17 + h], sb["vb_sb"][:, s, :],
                        OP.mult, OP.add),
                      reads=[PR[bk], res("stats"), res("vb_sb")], writes=[res("ybuf")])
                    E("tensor", lambda t, s=s, ub=ub, uc=uc: t.matmul(
                        pb[ub][:, uc:uc + 256], sb["P_sb"][:, s, :],
                        sb["ybuf"][:, s, :],
                        start=True, stop=True, skip_group_check=True),
                      reads=[res("P_sb"), res("ybuf")],
                      writes=[res(f"pu{s}")] + ([PR[7]] if ub == 7 else []))
                    E("scalar", lambda sc, s=s, ub=ub, uc=uc: sc.copy(
                        sb["U_pk"][:, s, :], pb[ub][:, uc:uc + 256]),
                      reads=[res(f"pu{s}")], writes=[res(f"upk{s}")])
                    E("vector", lambda v, s=s: v.tensor_tensor(
                        sb["A_sb"][:, s, :], sb["G_sb"][:, s, 128:256],
                        sb["c_trif"][:], OP.mult),
                      reads=[res("G_sb"), res("c_trif")], writes=[res("A_sb")])
                    E("tensor", lambda t, s=s, bk=bk: t.matmul(
                        pb[bk][:, 256:512], sb["A_sb"][:, s, :],
                        sb["U_pk"][:, s, :],
                        start=False, stop=False, skip_group_check=True),
                      reads=[res("A_sb"), res(f"upk{s}")], writes=[PR[bk]])
                if PROBE and g == 0 and h == 0:
                    E("gpsimd", lambda q: q.dma_start(out=dr["pr_G"], in_=sb["G_sb"][:]),
                      reads=[res("G_sb")], dma=True, sem="d_prb")
                    E("gpsimd", lambda q: q.dma_start(out=dr["pr_P"], in_=sb["P_sb"][:]),
                      reads=[res("P_sb")], dma=True, sem="d_prb")
                    E("gpsimd", lambda q: q.dma_start(out=dr["pr_U"], in_=sb["U_pk"][:]),
                      reads=[res("U_pk")], dma=True, sem="d_prb")
                for s in range(NSEQ):
                    bk = pxof[s]
                    E("vector", lambda v, s=s, h=h, bk=bk: v.tensor_scalar(
                        sb["d_tm"][:, s, 256 * h:256 * h + 256], pb[bk][:, 256:512],
                        stt[:, s, h:h + 1], None, OP.mult),
                      reads=[PR[bk], res("stats")], writes=[res("d_tm")])
                    for ki in range(2):
                        E("tensor", lambda t, s=s, h=h, ki=ki: t.matmul(
                            pb[7][:, 256 * ki:256 * ki + 256],
                            sb["k_tm"][:, s, 256 * h + 128 * ki:256 * h + 128 * ki + 128],
                            sb["U_pk"][:, s, :],
                            start=(ki == 0), stop=(ki == 1), skip_group_check=True),
                          reads=[res("k_tm"), res(f"upk{s}")],
                          writes=[PR[7]] + ([res("pu2"), res("pu3")]
                                            if (s == 0 and ki == 0) else []))
                    E("vector", lambda v, s=s, h=h: v.tensor_tensor(
                        sb["S32"][:, s * 4 + h, :], sb["S32"][:, s * 4 + h, :],
                        pb[7][:], OP.add),
                      reads=[PR[7], res("S32")], writes=[res("S32")])
                    E("scalar", lambda v, s=s, h=h: v.copy(
                        sb["S16"][:, s * 4 + h, :], sb["S32"][:, s * 4 + h, :]),
                      reads=[res("S32")], writes=[res("S16")])

            for mc in range(8):
                cslot = mc % 2
                ca = sb["cacc"][:, cslot, :, :]
                ca_res = res(f"cacc{cslot}")
                vv = vfm[:, mc, :, :]
                E("vector", lambda v, vv=vv, ca=ca, mc=mc: v.tensor_scalar(
                    ca, vv[:, :, 32:160], sb["w_fs"][:, mc, 2:3], None, OP.mult),
                  reads=[res("v_fm"), res("w_fs")], writes=[ca_res])
                for j in range(2):
                    sh = 2 - j
                    E("vector", lambda v, vv=vv, ca=ca, mc=mc, j=j, sh=sh:
                      v.scalar_tensor_tensor(
                          ca, vv[:, :, 32 - sh:160 - sh], sb["w_fs"][:, mc, j:j + 1],
                          ca, OP.mult, OP.add),
                      reads=[res("v_fm"), ca_res, res("w_fs")], writes=[ca_res])
                E("vector", lambda v, ca=ca, mc=mc: v.tensor_copy(
                    sb["fs_fm"][:, mc, :, :], ca),
                  reads=[ca_res], writes=[res("q_fm")])
            for c8 in range(8):
                E("vector", lambda v, c8=c8: v.tensor_copy(
                    vfm[:, c8, :, 0:32], vfm[:, c8, :, 128:160]),
                  reads=[res("v_fm")], writes=[res("v_fm")])

            for s in range(NSEQ):
                E("gpsimd", lambda q, s=s: q.dma_start(
                    out=sb["v_tl"][32 * s:32 * s + 30, :], in_=sb["v_tm"][98:128, s, :]),
                  reads=[res("v_tm")], writes=[res("v_tl")], dma=True, sem="d_vtl")

                bk = 2 + s % 2
                for c8 in range(8):
                    E("tensor", lambda t, s=s, c8=c8, bk=bk: t.transpose(
                        pb[bk][:, 128 * c8:128 * c8 + 128],
                        sb["d_tm"][:, s, 128 * c8:128 * c8 + 128], sb["c_idb"][:]),
                      reads=[res("d_tm"), res("c_idb")], writes=[PR[bk]])
                E("vector", lambda v, s=s, bk=bk: v.tensor_copy(
                    dfm[:, :, s, :], pb[bk][:].rearrange("p (c t) -> p c t", c=8)),
                  reads=[PR[bk]], writes=[res("v_fm")])
            for s in range(NSEQ):
                bk = 2 + (s + 1) % 2
                for c8 in range(8):
                    E("tensor", lambda t, s=s, c8=c8, bk=bk: t.transpose(
                        pb[bk][:, 128 * c8:128 * c8 + 128],
                        sb["fl_tm"][:, s, 128 * c8:128 * c8 + 128], sb["c_idb"][:]),
                      reads=[res("fl_tm"), res("c_idb")], writes=[PR[bk]])
                E("vector", lambda v, s=s, bk=bk: v.tensor_copy(
                    sb["fl_fm"][:, :, s, :], pb[bk][:].rearrange("p (c t) -> p c t", c=8)),
                  reads=[PR[bk]], writes=[res("k_fm")])

            gate_res = [res("xn_fm"), res("q_fm"), res("k_fm"), res("v_fm")]
            gate_srcs = [xfm, sb["fs_fm"][:], sb["fl_fm"][:], dfm]
            first_f2 = [True]
            for quarter in range(4):
                for kc in range(32):
                    if kc % 4 == 0:
                        li = quarter * 8 + kc // 4
                        ctr["w1"] += 1
                    slot = (grp_w1_base + li) % 2
                    src = gate_srcs[kc // 8]
                    kcc = kc % 8
                    for mc4 in range(4):
                        fbk = (0, 1, 4, 5)[mc4]
                        if kc // 8 == 3:
                            for s4 in range(NSEQ):
                                E("tensor", lambda t, kc=kc, kcc=kcc, mc4=mc4, slot=slot,
                                  src=src, fbk=fbk, s4=s4:
                                  t.matmul(
                                      pb[fbk][:, 128 * s4:128 * s4 + 128],
                                      sb["rg_w1"][:, slot,
                                                  512 * (kc % 4) + 128 * mc4:
                                                  512 * (kc % 4) + 128 * mc4 + 128],
                                      src[:, kcc, s4, :],
                                      start=False, stop=(kc == 31 and s4 == 3),
                                      skip_group_check=True),
                                  reads=[res(f"rg1{slot}"), gate_res[kc // 8]],
                                  writes=[PR[fbk]])
                        else:
                            E("tensor", lambda t, kc=kc, kcc=kcc, mc4=mc4, slot=slot,
                              src=src, fbk=fbk:
                              t.matmul(
                                  pb[fbk][:],
                                  sb["rg_w1"][:, slot,
                                              512 * (kc % 4) + 128 * mc4:
                                              512 * (kc % 4) + 128 * mc4 + 128],
                                  src[:, kcc, :, :].rearrange("p s t -> p (s t)"),
                                  start=(kc == 0), stop=False, skip_group_check=True),
                              reads=[res(f"rg1{slot}"), gate_res[kc // 8]],
                              writes=[PR[fbk]])
                    if kc % 4 == 3 and li + 2 < 32:
                        emit_rg1(li + 2)
                for mc4 in range(4):
                    fbk = (0, 1, 4, 5)[mc4]
                    mc = quarter * 4 + mc4
                    hs = mc % 2
                    E("scalar", lambda s_, mc=mc, fbk=fbk, hs=hs: s_.activation(
                        sb["hidr"][:, hs, :], pb[fbk][:], AF.Gelu,
                        bias=sb["w_b1"][:, mc:mc + 1]),
                      reads=[PR[fbk], res("w_b1")], writes=[res(f"hid{hs}")])
                    stf = first_f2[0]
                    first_f2[0] = False
                    E("tensor", lambda t, mc=mc, hs=hs, stf=stf: t.matmul(
                        pb[7][0:16, :], sb["w_w2"][:, mc, :], sb["hidr"][:, hs, :],
                        start=stf, stop=(mc == 15), skip_group_check=True),
                      reads=[res(f"hid{hs}"), res("w_w2")], writes=[PR[7]])

            E("vector", lambda v: v.tensor_scalar(
                sb["slab"][0:16, :], pb[7][0:16, :], sb["w_b2"][0:16, 0:1], None, OP.add),
              reads=[PR[7], res("w_b2")], writes=[res("slab")])
            E("vector", lambda v: v.tensor_scalar(
                sb["slab"][0:12, :], sb["slab"][0:12, :], sb["taurec"][0:12, 0:1],
                None, OP.mult),
              reads=[res("slab"), res("taurec")], writes=[res("slab")])
            for s in range(NSEQ):
                E("tensor", lambda t, s=s: t.transpose(
                    pb[6][:, 0:32], sb["slab"][0:32, 128 * s:128 * s + 128],
                    sb["c_idf"][0:32, 0:32]),
                  reads=[res("slab"), res("c_idf")], writes=[PR[6]])
                E("vector", lambda v, s=s: v.tensor_copy(stt[:, s, 20:36], pb[6][:, 0:16]),
                  reads=[PR[6]], writes=[res("stats")])
                E("scalar", lambda s_, s=s: s_.activation(
                    stt[:, s, 36:40], stt[:, s, 32:36], AF.Sigmoid),
                  reads=[res("stats")], writes=[res("stats")])
                E("scalar", lambda s_, s=s: s_.activation(
                    stt[:, s, 20:32], stt[:, s, 20:32], AF.Exp),
                  reads=[res("stats")], writes=[res("stats")])
                E("vector", lambda v, s=s: v.tensor_tensor(
                    sb["gt4"][:, 0:4], stt[:, s, 20:24], stt[:, s, 24:28], OP.add),
                  reads=[res("stats")], writes=[res("gt4")])
                E("vector", lambda v, s=s: v.tensor_tensor(
                    sb["gt4"][:, 0:4], sb["gt4"][:, 0:4], stt[:, s, 28:32], OP.add),
                  reads=[res("gt4"), res("stats")], writes=[res("gt4")])
                E("vector", lambda v: v.reciprocal(sb["gt4"][:, 0:4], sb["gt4"][:, 0:4]),
                  reads=[res("gt4")], writes=[res("gt4")])
                E("vector", lambda v, s=s: v.tensor_scalar(
                    sb["gt4"][:, 4:8], stt[:, s, 36:40], 0.09, 0.9, OP.mult, OP.add),
                  reads=[res("stats")], writes=[res("gt4")])
                E("vector", lambda v, s=s: v.tensor_tensor(
                    stt[:, s, 36:40], sb["gt4"][:, 4:8], stt[:, s, 36:40], OP.mult),
                  reads=[res("gt4"), res("stats")], writes=[res("stats")])
                E("vector", lambda v, s=s: v.tensor_scalar(
                    sb["gt4"][:, 4:8], stt[:, s, 36:40], -1.0, 1.0, OP.mult, OP.add),
                  reads=[res("stats")], writes=[res("gt4")])
                E("vector", lambda v: v.tensor_tensor(
                    sb["gt4"][:, 4:8], sb["gt4"][:, 4:8], sb["gt4"][:, 0:4], OP.mult),
                  reads=[res("gt4")], writes=[res("gt4")])
                for gix in range(3):
                    E("vector", lambda v, s=s, gix=gix: v.tensor_tensor(
                        stt[:, s, 20 + 4 * gix:24 + 4 * gix],
                        stt[:, s, 20 + 4 * gix:24 + 4 * gix], sb["gt4"][:, 4:8], OP.mult),
                      reads=[res("stats"), res("gt4")], writes=[res("stats")])

            if PROBE and g == 0:
                E("gpsimd", lambda q: q.dma_start(out=dr["pr_stats"], in_=stt[:]),
                  reads=[res("stats")], dma=True, sem="d_prb")
                E("gpsimd", lambda q: q.dma_start(out=dr["pr_slab"], in_=sb["slab"][:]),
                  reads=[res("slab")], dma=True, sem="d_prb")

            for s in range(NSEQ):
                bk = 2 + s % 2
                for c8 in range(8):
                    E("tensor", lambda t, s=s, c8=c8, bk=bk: t.transpose(
                        pb[bk][:, 128 * c8:128 * c8 + 128],
                        sb["fs_fm"][:, c8, s, :], sb["c_idb"][:]),
                      reads=[res("q_fm"), res("c_idb")], writes=[PR[bk]])
                E("vector", lambda v, bk=bk: v.tensor_copy(
                    sb["fstm"][:].rearrange("p a b -> p (a b)"), pb[bk][:]),
                  reads=[PR[bk]], writes=[res("ybuf")])
                for h in range(4):
                    hsl = slice(256 * h, 256 * h + 256)
                    E("vector", lambda v, s=s, h=h, hsl=hsl: v.tensor_scalar(
                        sb["o_tm"][:].rearrange("p a b -> p (a b)")[:, hsl], sb["v_tm"][:, s, hsl],
                        stt[:, s, 36 + h:37 + h], None, OP.mult),
                      reads=[res("v_tm"), res("stats")], writes=[res("U_pk"), res("upk0"), res("upk1"), res("upk2"), res("upk3")])
                    E("vector", lambda v, s=s, h=h, hsl=hsl: v.scalar_tensor_tensor(
                        sb["o_tm"][:].rearrange("p a b -> p (a b)")[:, hsl], sb["d_tm"][:, s, hsl],
                        stt[:, s, 28 + h:29 + h], sb["o_tm"][:].rearrange("p a b -> p (a b)")[:, hsl], OP.mult, OP.add),
                      reads=[res("d_tm"), res("stats"), res("U_pk")], writes=[res("U_pk")])
                    E("vector", lambda v, s=s, h=h, hsl=hsl: v.scalar_tensor_tensor(
                        sb["o_tm"][:].rearrange("p a b -> p (a b)")[:, hsl], sb["fl_tm"][:, s, hsl],
                        stt[:, s, 24 + h:25 + h], sb["o_tm"][:].rearrange("p a b -> p (a b)")[:, hsl], OP.mult, OP.add),
                      reads=[res("fl_tm"), res("stats"), res("U_pk")], writes=[res("U_pk")])
                    E("vector", lambda v, s=s, h=h, hsl=hsl: v.scalar_tensor_tensor(
                        sb["o_tm"][:].rearrange("p a b -> p (a b)")[:, hsl], sb["fstm"][:].rearrange("p a b -> p (a b)")[:, hsl],
                        stt[:, s, 20 + h:21 + h], sb["o_tm"][:].rearrange("p a b -> p (a b)")[:, hsl], OP.mult, OP.add),
                      reads=[res("ybuf"), res("stats"), res("U_pk")], writes=[res("U_pk")])
                    E("vector", lambda v, h=h, hsl=hsl: v.scalar_tensor_tensor(
                        sb["vb_sb"][:, h, :], sb["o_tm"][:].rearrange("p a b -> p (a b)")[:, hsl], 1.0, sb["o_tm"][:].rearrange("p a b -> p (a b)")[:, hsl],
                        OP.mult, OP.mult, accum_out=sb["o2c"][:, h:h + 1]),
                      reads=[res("U_pk")], writes=[res("o2c"), res("vb_sb")])
                if PROBE and g == 0:
                    E("gpsimd", lambda q, s=s: q.dma_start(
                        out=dr["pr_otm"][s],
                        in_=sb["o_tm"][:].rearrange("p a b -> p (a b)")),
                      reads=[res("U_pk")], dma=True, sem="d_prb")
                E("scalar", lambda s_: s_.activation(sb["o2c"][:, 0:4], sb["o2c"][:, 0:4],
                                                     AF.Sqrt, bias=sb["epsA"][:, 0:1],
                                                     scale=1.0 / 256.0),
                  reads=[res("o2c"), res("epsA")], writes=[res("o2c")])
                E("vector", lambda v: v.reciprocal(sb["o2c"][:, 0:4], sb["o2c"][:, 0:4]),
                  reads=[res("o2c")], writes=[res("o2c")])
                for h in range(4):
                    hsl = slice(256 * h, 256 * h + 256)
                    E("vector", lambda v, h=h, hsl=hsl: v.tensor_scalar(
                        sb["orms"][:, hsl], sb["o_tm"][:].rearrange("p a b -> p (a b)")[:, hsl], sb["o2c"][:, h:h + 1],
                        None, OP.mult),
                      reads=[res("U_pk"), res("o2c")], writes=[res("aq_tb")])
                if PROBE and g == 0:
                    E("gpsimd", lambda q, s=s: q.dma_start(
                        out=dr["pr_orms"][s], in_=sb["orms"][:]),
                      reads=[res("aq_tb")], dma=True, sem="d_prb")
                bko = 2 + (s + 1) % 2
                for c8 in range(8):
                    E("tensor", lambda t, c8=c8, bko=bko: t.transpose(
                        pb[bko][:, 128 * c8:128 * c8 + 128],
                        sb["orms"][:, 128 * c8:128 * c8 + 128], sb["c_idb"][:]),
                      reads=[res("aq_tb"), res("c_idb")], writes=[PR[bko]])
                E("vector", lambda v, s=s, bko=bko: v.tensor_copy(
                    sb["d_tm"][:, s, :], pb[bko][:]),
                  reads=[PR[bko]], writes=[res("d_tm")])
                for kc in range(8):
                    slot = kc // 4
                    kbk = (0, 1, 4, 5)[s]
                    E("tensor", lambda t, s=s, kc=kc, slot=slot, kbk=kbk: t.matmul(
                        pb[kbk][:], sb["d_tm"][:, s, 128 * kc:128 * kc + 128],
                        sb["rg_wo"][:, slot, 512 * (kc % 4):512 * (kc % 4) + 512],
                        start=(kc == 0), stop=(kc == 7), skip_group_check=True),
                      reads=[res("d_tm"), res(f"rgo{slot}")], writes=[PR[kbk]])
                xslot = s % 2
                dma("sync", sb["xbuf"][:, xslot, :], x_h[s, t0:t0 + GT, :],
                    f"d_xb{xslot}", writes=[res(f"xbuf{xslot}")])
                tbuf = sb["sc_a"] if s < 2 else sb["xn_tm"]
                tr_res = res("sc_a" if s < 2 else "xn_tm")
                kbk = (0, 1, 4, 5)[s]
                E("vector", lambda v, s=s, xslot=xslot, tbuf=tbuf, kbk=kbk:
                  v.tensor_tensor(
                    tbuf[:, 512 * (s % 2):512 * (s % 2) + 512], pb[kbk][:],
                    sb["xbuf"][:, xslot, 0:512], OP.add),
                  reads=[PR[kbk], res(f"xbuf{xslot}")], writes=[tr_res])

            for half in range(2):
                if half == 0:
                    for b in (0, 1):
                        slot = ctr["wo"] % 2
                        ctr["wo"] += 1
                        dma("sync",
                            sb["rg_wo"][:, slot, :].rearrange("p (b f) -> p b f", b=4),
                            wf_b4[b], f"d_rgo{slot}",
                            reads=[res("hbm_wft_b")], writes=[res(f"rgo{slot}")])
                else:
                    for b in (2, 3):
                        slot = ctr["wo"] % 2
                        ctr["wo"] += 1
                        dma("sync",
                            sb["rg_wo"][:, slot, :].rearrange("p (b f) -> p b f", b=4),
                            wo_b4[b], f"d_rgo{slot}",
                            reads=[res("hbm_wo_b")], writes=[res(f"rgo{slot}")])
                    for s in range(NSEQ):
                        for kc in range(8):
                            slot = kc // 4
                            kbk = (0, 1, 4, 5)[s]
                            E("tensor", lambda t, s=s, kc=kc, slot=slot, kbk=kbk:
                              t.matmul(
                                pb[kbk][:], sb["d_tm"][:, s, 128 * kc:128 * kc + 128],
                                sb["rg_wo"][:, slot,
                                            512 * (kc % 4):512 * (kc % 4) + 512],
                                start=(kc == 0), stop=(kc == 7),
                                skip_group_check=True),
                              reads=[res("d_tm"), res(f"rgo{slot}")], writes=[PR[kbk]])
                        xslot = s % 2
                        dma("sync", sb["xbuf"][:, xslot, :], x_h[s, t0:t0 + GT, :],
                            f"d_xb{xslot}", writes=[res(f"xbuf{xslot}")])
                        tbuf = sb["sc_a"] if s < 2 else sb["xn_tm"]
                        tr_res = res("sc_a" if s < 2 else "xn_tm")
                        kbk = (0, 1, 4, 5)[s]
                        E("vector", lambda v, s=s, xslot=xslot, tbuf=tbuf, kbk=kbk:
                          v.tensor_tensor(
                            tbuf[:, 512 * (s % 2):512 * (s % 2) + 512], pb[kbk][:],
                            sb["xbuf"][:, xslot, 512:1024], OP.add),
                          reads=[PR[kbk], res(f"xbuf{xslot}")], writes=[tr_res])
                    for b in (2, 3):
                        slot = ctr["wo"] % 2
                        ctr["wo"] += 1
                        dma("sync",
                            sb["rg_wo"][:, slot, :].rearrange("p (b f) -> p b f", b=4),
                            wf_b4[b], f"d_rgo{slot}",
                            reads=[res("hbm_wft_b")], writes=[res(f"rgo{slot}")])
                for s in range(NSEQ):
                    for kc in range(8):
                        slot = kc // 4
                        kbk = (0, 1, 4, 5)[s]
                        E("tensor", lambda t, s=s, kc=kc, half=half, slot=slot, kbk=kbk:
                          t.matmul(
                            pb[kbk][:], afm[:, kc, s, :],
                            sb["rg_wo"][:, slot, 512 * (kc % 4):512 * (kc % 4) + 512],
                            start=(kc == 0), stop=(kc == 7), skip_group_check=True),
                          reads=[res("aq_fm"), res(f"rgo{slot}")], writes=[PR[kbk]])
                    oslot = s % 2
                    tbuf = sb["sc_a"] if s < 2 else sb["xn_tm"]
                    tr_res = res("sc_a" if s < 2 else "xn_tm")
                    kbk = (0, 1, 4, 5)[s]
                    E("vector", lambda v, s=s, half=half, oslot=oslot, tbuf=tbuf, kbk=kbk:
                      v.scalar_tensor_tensor(
                          sb["outb"][:, oslot, :],
                          pb[kbk][:], sb["sffn"][:, s:s + 1],
                          tbuf[:, 512 * (s % 2):512 * (s % 2) + 512], OP.mult, OP.add),
                      reads=[PR[kbk], res("sffn"), tr_res],
                      writes=[res(f"outb{oslot}")])
                    dma("gpsimd", out_h[s, t0:t0 + GT, 512 * half:512 * half + 512],
                        sb["outb"][:, oslot, :], f"d_out{oslot}",
                        reads=[res(f"outb{oslot}")], writes=[res(f"outd{oslot}")])

        cum = P.finalize()

        @block.sync
        def _(q):
            P.replay("sync", q, sems, cum)

        @block.tensor
        def _(t):
            P.replay("tensor", t, sems, cum)

        @block.vector
        def _(v):
            P.replay("vector", v, sems, cum)

        @block.scalar
        def _(s_):
            P.replay("scalar", s_, sems, cum)

        @block.gpsimd
        def _(q):
            P.replay("gpsimd", q, sems, cum)
            for k in sorted(P.tot):
                if k.startswith("d_") and k in sems:
                    q.wait_ge(sems[k], P.tot[k])

    return nc


_CACHED = {}


def kernel(**inputs):
    x = np.ascontiguousarray(np.asarray(inputs["L0_attn_hidden_states"], np.float32))
    B, L, _ = x.shape
    nper = B // NCORES

    w = _prep_weights(inputs)
    c = _host_consts()

    key = ("main", L)
    if key not in _CACHED:
        _CACHED[key] = build_program(L=L)
    nc = _CACHED[key]

    base = {}
    base.update({k: np.ascontiguousarray(v) for k, v in w.items()})
    base.update({k: np.ascontiguousarray(v) for k, v in c.items()})

    in_maps = []
    for ci in range(NCORES):
        m = dict(base)
        m["x"] = np.ascontiguousarray(x[ci * nper:(ci + 1) * nper])
        in_maps.append(m)

    res_ = run_bass_kernel_spmd(nc, in_maps, core_ids=list(range(NCORES)))
    out = np.concatenate([r["out"] for r in res_.results], axis=0)
    return out


# version 18 (final)
# speedup vs baseline: 1.0310x; 1.0310x over previous
import numpy as np
import ml_dtypes

import concourse.bass as bass
import concourse.mybir as mybir
from concourse.bass_utils import run_bass_kernel_spmd

F32 = mybir.dt.float32
BF16 = mybir.dt.bfloat16
I8 = mybir.dt.int8
OP = mybir.AluOpType
AF = mybir.ActivationFunctionType
AX = mybir.AxisListType

NCORES = 8
H, DK, DV, D = 4, 256, 256, 1024
GT = 128
NSEQ = 4
NFREQ = 81
NDFT = 160

BF = np.dtype(ml_dtypes.bfloat16)


class Res:
    __slots__ = ("name", "writer", "readers")

    def __init__(self, name):
        self.name = name
        self.writer = None
        self.readers = []


class Plan:

    ENGINES = ("sync", "tensor", "vector", "scalar", "gpsimd")

    def __init__(self):
        self.ops = {e: [] for e in self.ENGINES}
        self.n = {e: 0 for e in self.ENGINES}
        self.barrier = set()

    def emit(self, eng, fn, reads=(), writes=(), dma=False, sem=None):
        idx = self.n[eng]
        handle = (eng, idx)
        waits = set()
        st_set = getattr(self, "strict", False)
        if st_set is True:
            st_set = set(self.ENGINES)
        elif st_set is False:
            st_set = set()
        strict = eng in st_set
        for r in reads:
            if r.writer is not None and (strict or r.writer[0] != eng):
                waits.add(r.writer)
        for r in writes:
            if r.writer is not None and (strict or r.writer[0] != eng):
                waits.add(r.writer)
            for rd in r.readers:
                if strict or rd[0] != eng:
                    waits.add(rd)
        for r in reads:
            r.readers.append(handle)
        for r in writes:
            r.writer = handle
            r.readers = []
        if dma and sem is None:
            raise RuntimeError("DMA ops must name a semaphore chain")
        self.ops[eng].append([fn, waits, False, dma, sem or eng])
        self.n[eng] = idx + 1
        return handle

    def finalize(self):
        for eng in self.ENGINES:
            for op in self.ops[eng]:
                if op[3]:
                    op[2] = True
                for (peng, pidx) in op[1]:
                    self.ops[peng][pidx][2] = True
        cum = {}
        tot = {}
        per_sem = {}
        for eng in self.ENGINES:
            for i, op in enumerate(self.ops[eng]):
                if op[2]:
                    k = op[4]
                    per_sem[k] = per_sem.get(k, 0) + (16 if op[3] else 1)
                    cum[(eng, i)] = (k, per_sem[k])
        tot.update(per_sem)
        self.cum, self.tot = cum, tot
        return cum

    def replay(self, eng_name, eng, sems, cum):
        seen = {}
        for i, (fn, waits, marked, dma, semkey) in enumerate(self.ops[eng_name]):
            need = {}
            for h in waits:
                if h not in self.cum:
                    continue
                k, v = self.cum[h]
                if k in self.barrier:
                    v = self.tot[k]
                if v > need.get(k, -1):
                    need[k] = v
            for k in sorted(need):
                if seen.get(k, -1) >= need[k]:
                    continue
                seen[k] = need[k]
                eng.wait_ge(sems[k], need[k])
            inst = fn(eng)
            if marked:
                if inst is None:
                    raise RuntimeError(f"marked op returned None on {eng_name}")
                inst.then_inc(sems[semkey], 16 if dma else 1)


def _host_consts():
    c = {}
    c["id_f32"] = np.eye(128, dtype=np.float32)
    c["id_bf"] = np.eye(128, dtype=BF)
    idx = np.arange(128)
    blk = idx // 32
    sameblk = blk[:, None] == blk[None, :]
    c["mask_A"] = np.where(idx[:, None] > idx[None, :], 1.0, 0.0).astype(BF)
    c["trif"] = (idx[:, None] <= idx[None, :]).astype(np.float32).astype(BF)
    i32 = np.arange(32)
    c["tri32"] = np.tile((i32[:, None] <= i32[None, :]).astype(np.float32), (4, 1)).astype(BF)
    c["tri32x4"] = np.tile((i32[:, None] <= i32[None, :]).astype(np.float32), (4, 4)).astype(BF)
    selpad = np.zeros((128, 65), np.float32)
    selpad[:, 32] = 1.0
    c["selpad"] = selpad.astype(BF)
    c["ones_row"] = np.ones((1, 128), np.float32)
    t_all = np.arange(NDFT)
    f = np.arange(NFREQ)
    ang = 2.0 * np.pi * np.outer(t_all, f) / NDFT
    Dr, Di = np.cos(ang), -np.sin(ang)
    c["dft_cur_r"] = Dr[30:158].astype(BF)
    c["dft_cur_i"] = Di[30:158].astype(BF)
    pr = np.zeros((32, NFREQ)); pr[:30] = Dr[0:30]
    pi_ = np.zeros((32, NFREQ)); pi_[:30] = Di[0:30]
    c["dft_prev_r"] = np.tile(pr, (4, 1)).astype(BF)
    c["dft_prev_i"] = np.tile(pi_, (4, 1)).astype(BF)
    cf = np.full(NFREQ, 2.0); cf[0] = 1.0; cf[80] = 1.0
    tout = np.arange(30, 158)
    ango = 2.0 * np.pi * np.outer(f, tout) / NDFT
    c["idft_r"] = ((cf[:, None] / NDFT) * np.cos(ango)).astype(BF)
    c["idft_i"] = (-(cf[:, None] / NDFT) * np.sin(ango)).astype(BF)
    K = 31
    lag = (K - 1) - np.arange(K)
    angw = 2.0 * np.pi * np.outer(lag, f) / NDFT
    er = np.zeros((32, NFREQ)); er[:31] = np.cos(angw)
    ei = np.zeros((32, NFREQ)); ei[:31] = -np.sin(angw)
    c["er"] = er.astype(BF)
    c["ei"] = ei.astype(BF)
    return c


def _prep_weights(inputs):
    w = {}

    def tile64(W):
        return np.ascontiguousarray(
            W.reshape(8, 128, 8, 128).transpose(2, 0, 1, 3).reshape(64, 128, 128))

    w["wq_t"] = tile64(np.asarray(inputs["L0_attn_W_q"], np.float32))
    w["wk_t"] = tile64(np.asarray(inputs["L0_attn_W_k"], np.float32))
    w["wv_t"] = tile64(np.asarray(inputs["L0_attn_W_v"], np.float32))
    W1 = np.asarray(inputs["L0_attn_W_fuse1"], np.float32)
    w["w1_t"] = np.ascontiguousarray(
        W1.reshape(32, 128, 4, 512).transpose(2, 0, 1, 3).reshape(128, 128, 512))
    Wo = np.asarray(inputs["L0_attn_W_o"], np.float32)
    w["wo_t"] = np.ascontiguousarray(
        Wo.reshape(8, 128, 2, 512).transpose(2, 0, 1, 3).reshape(16, 128, 512))
    Wf = np.asarray(inputs["L0_ffn_weight"], np.float32)
    WfT = np.ascontiguousarray(Wf.T)
    w["wft_t"] = np.ascontiguousarray(
        WfT.reshape(8, 128, 2, 512).transpose(2, 0, 1, 3).reshape(16, 128, 512))
    Wb = np.asarray(inputs["L0_attn_W_b"], np.float32)
    wbp = np.zeros((1024, 32), np.float32)
    wbp[:, 8:12] = Wb
    w["wb_t"] = np.ascontiguousarray(wbp.reshape(8, 128, 32))
    W2 = np.asarray(inputs["L0_attn_W_fuse2"], np.float32)
    perm = np.array([h * 4 + g for g in range(4) for h in range(4)])
    w["w2_t"] = np.ascontiguousarray(W2[:, perm].reshape(16, 128, 16))
    b2 = np.asarray(inputs["L0_attn_b_fuse2"], np.float32)
    w["b2_p"] = np.ascontiguousarray(b2[perm].reshape(16, 1))
    b1 = np.asarray(inputs["L0_attn_b_fuse1"], np.float32)
    w["b1_c"] = np.ascontiguousarray(b1.reshape(16, 128).T)
    w["lt12"] = np.ascontiguousarray(
        np.tile(np.asarray(inputs["L0_attn_others_log_tau"], np.float32), 3).reshape(12, 1))
    for nm, key in (("cw_q", "L0_attn_conv_w_q"), ("cw_k", "L0_attn_conv_w_k"),
                    ("cw_v", "L0_attn_conv_w_v")):
        cw = np.asarray(inputs[key], np.float32)
        w[nm] = np.ascontiguousarray(cw.reshape(8, 128, 4).transpose(1, 0, 2))
    fs = np.asarray(inputs["L0_attn_fir_short_w"], np.float32).reshape(1024, 3)
    w["fir_s"] = np.ascontiguousarray(fs.reshape(8, 128, 3).transpose(1, 0, 2))
    fl = np.asarray(inputs["L0_attn_fir_long_w"], np.float32).reshape(1024, 31)
    flT = np.zeros((32, 1024), np.float32)
    flT[:31] = fl.T
    w["firl_T"] = flT
    w["nw_t"] = np.ascontiguousarray(
        np.tile(np.asarray(inputs["L0_attn_norm_w"], np.float32), 4).reshape(8, 128).T)
    return w


PROBE = False
STRICT = {"vector"}
DETECT = False


def build_program(L=2048):
    NG = L // GT
    nc = bass.Bass("TRN2", target_bir_lowering=False,
                   detect_race_conditions=DETECT)
    P = Plan()
    P.strict = STRICT

    dr = {}

    def dt_in(name, shape, dt):
        dr[name] = nc.dram_tensor(name, list(shape), dt, kind="ExternalInput").ap()
        return dr[name]

    def dt_int(name, shape, dt):
        dr[name] = nc.dram_tensor(name, list(shape), dt, kind="Internal").ap()
        return dr[name]

    x_h = dt_in("x", (NSEQ, L, D), F32)
    out_h = nc.dram_tensor("out", [NSEQ, L, D], F32, kind="ExternalOutput").ap()

    for nm, shape, dt in [
        ("wq_t", (64, 128, 128), F32), ("wk_t", (64, 128, 128), F32),
        ("wv_t", (64, 128, 128), F32), ("w1_t", (128, 128, 512), F32),
        ("wo_t", (16, 128, 512), F32), ("wft_t", (16, 128, 512), F32),
        ("wb_t", (8, 128, 32), F32), ("w2_t", (16, 128, 16), F32),
        ("b2_p", (16, 1), F32), ("b1_c", (128, 16), F32), ("lt12", (12, 1), F32),
        ("cw_q", (128, 8, 4), F32), ("cw_k", (128, 8, 4), F32), ("cw_v", (128, 8, 4), F32),
        ("fir_s", (128, 8, 3), F32), ("firl_T", (32, 1024), F32), ("nw_t", (128, 8), F32),
        ("id_f32", (128, 128), F32), ("id_bf", (128, 128), BF16),
        ("mask_A", (128, 128), BF16), ("tri32", (128, 32), BF16),
        ("tri32x4", (128, 128), BF16), ("trif", (128, 128), BF16),
        ("selpad", (128, 65), BF16), ("ones_row", (1, 128), F32),
        ("dft_cur_r", (128, 81), BF16), ("dft_cur_i", (128, 81), BF16),
        ("dft_prev_r", (128, 81), BF16), ("dft_prev_i", (128, 81), BF16),
        ("idft_r", (81, 128), BF16), ("idft_i", (81, 128), BF16),
        ("er", (32, 81), BF16), ("ei", (32, 81), BF16),
    ]:
        dt_in(nm, shape, dt)

    if PROBE:
        def dt_out(name, shape, dt):
            dr[name] = nc.dram_tensor(name, list(shape), dt, kind="ExternalOutput").ap()
        dt_out("pr_dtm", (128, NSEQ, 1024), F32)
        dt_out("pr_stats", (128, NSEQ, 40), F32)
        dt_out("pr_otm", (NSEQ, 128, 1024), F32)
        dt_out("pr_orms", (NSEQ, 128, 1024), F32)
        dt_out("pr_slab", (32, 512), F32)
        dt_out("pr_xnfm", (128, 8, NSEQ, 128), BF16)
        dt_out("pr_qfm", (128, 8, NSEQ, 128), BF16)
        dt_out("pr_kfm", (128, 8, NSEQ, 128), BF16)
        dt_out("pr_vfm", (128, 8, NSEQ, 160), BF16)
        dt_out("pr_G", (128, NSEQ, 256), BF16)
        dt_out("pr_P", (128, NSEQ, 128), BF16)
        dt_out("pr_U", (128, NSEQ, 256), BF16)
        dt_out("pr_flfm", (128, 8, NSEQ, 128), BF16)
        dt_out("pr_fsfm", (128, 8, NSEQ, 128), BF16)
        dt_out("pr_A", (128, NSEQ, 128), BF16)
        dt_out("pr_B", (128, NSEQ, 128), BF16)
        dt_out("pr_P1", (128, NSEQ, 128), BF16)
        dt_out("pr_A2", (4, 128, NSEQ, 128), BF16)
        dt_out("pr_Pl", (4, 128, NSEQ, 128), BF16)
    dt_int("wq_b", (64, 128, 128), BF16)
    dt_int("wk_b", (64, 128, 128), BF16)
    dt_int("wv_b", (64, 128, 128), BF16)
    dt_int("w1_b", (128, 128, 512), BF16)
    dt_int("wo_b", (16, 128, 512), BF16)
    dt_int("wft_b", (16, 128, 512), BF16)

    import contextlib
    es = contextlib.ExitStack()
    sb = {}

    def S(name, shape, dt):
        sb[name] = es.enter_context(nc.sbuf_tensor(name, list(shape), dt))
        return sb[name]

    with es:
        S("c_idf", (128, 128), F32); S("c_idb", (128, 128), BF16)
        S("c_mA", (128, 128), BF16); S("c_tri", (128, 32), BF16)
        S("c_tri4", (128, 128), BF16); S("c_trif", (128, 128), BF16)
        S("c_sel", (128, 65), BF16); S("c_or", (1, 128), F32)
        S("c_dcr", (128, 81), BF16); S("c_dci", (128, 81), BF16)
        S("c_dpr", (128, 81), BF16); S("c_dpi", (128, 81), BF16)
        S("c_ir", (81, 128), BF16); S("c_ii", (81, 128), BF16)
        S("c_er", (32, 81), BF16); S("c_ei", (32, 81), BF16)
        S("w_wb", (128, 8, 32), BF16)
        S("w_w2", (128, 16, 16), BF16)
        S("w_b1", (128, 16), F32); S("w_b2", (16, 1), F32)
        S("w_cwq", (128, 8, 4), F32); S("w_cwk", (128, 8, 4), F32)
        S("w_cwv", (128, 8, 4), F32); S("w_fs", (128, 8, 3), F32)
        S("w_nw", (128, 8), F32)
        S("taurec", (12, 1), F32)
        S("epsA", (128, 1), F32); S("epsB", (128, 1), F32); S("epsC", (128, 1), F32)
        S("whr", (128, 1024), BF16); S("whi", (128, 1024), BF16)
        S("ws127", (128, 1), F32); S("rws", (128, 1), F32)
        S("wcol", (128, 2), F32)
        S("rg_w", (128, 2, 1024), BF16)
        S("rg_w1", (128, 4, 1024), BF16)
        S("rg_wo", (128, 2, 2048), BF16)
        S("xbuf", (128, 2, 1024), F32)
        S("lnc", (128, 16), F32)
        S("sffn", (128, 4), F32)
        S("xn_tm", (128, 1024), F32)
        S("sc_a", (128, 1024), F32)
        S("aq8", (128, 1024), I8)
        S("aq_tb", (128, 1024), BF16)
        S("slab2", (8, 512), F32)
        S("xn_fm", (128, 8, NSEQ, 128), BF16)
        S("aq_fm", (128, 8, NSEQ, 128), BF16)
        S("q_fm", (128, 8, NSEQ, 128), BF16)
        S("k_fm", (128, 8, NSEQ, 128), BF16)
        S("v_fm", (128, 8, NSEQ, 160), BF16)
        S("prebuf", (128, 2, NSEQ, 132), BF16)
        S("phal", (128, 3, 8, NSEQ, 4), BF16)
        S("cacc", (128, 2, NSEQ, 128), BF16)
        S("q2b", (128, 2, 512), BF16)
        S("k_tm", (128, NSEQ, 1024), BF16)
        S("v_tm", (128, NSEQ, 1024), BF16)
        S("v_tl", (128, 1024), BF16)
        S("d_tm", (128, NSEQ, 1024), BF16)
        S("slab", (32, 512), F32)
        S("stats", (128, NSEQ, 40), F32)
        S("G_sb", (128, NSEQ, 256), BF16)
        S("A_sb", (128, NSEQ, 128), BF16)
        S("B_sb", (128, NSEQ, 128), BF16)
        S("P_sb", (128, NSEQ, 128), BF16)
        S("loc_sb", (128, NSEQ, 32), BF16)
        S("ybuf", (128, NSEQ, 256), BF16)
        S("vb_sb", (128, NSEQ, 256), BF16)
        S("U_pk", (128, NSEQ, 256), BF16)
        S("S32", (128, 16, 512), F32)
        S("S16", (128, 16, 512), BF16)
        S("hidr", (128, 2, 512), BF16)
        S("xh", (128, 2, 512), BF16)
        S("yh", (128, 4, 512), BF16)
        S("o2c", (128, 4), F32)
        S("gt4", (128, 8), F32)
        S("outb", (128, 2, 512), F32)

        pb = []
        for i in range(8):
            if i in (2, 3):
                pb.append(es.enter_context(nc.psum_tensor(f"pb{i}", [128, 1024], BF16)))
            else:
                pb.append(es.enter_context(nc.psum_tensor(f"pb{i}", [128, 512], F32)))

        sb["fs_fm"] = sb["q_fm"]
        sb["o_tm"] = sb["U_pk"]
        sb["fstm"] = sb["ybuf"]
        sb["fl_fm"] = sb["k_fm"]
        S("fl_tm", (128, NSEQ, 1024), BF16)
        sb["orms"] = sb["aq_tb"]

        sems = {}
        SEM_KEYS = list(Plan.ENGINES) + [
            "d_cst", "d_stg", "d_pre0", "d_pre1", "d_xb0", "d_xb1",
            "d_rgw0", "d_rgw1", "d_rgw2", "d_rg10", "d_rg11", "d_rg12", "d_rg13",
            "d_rgo0", "d_rgo1", "d_out0", "d_out1", "d_vtl", "d_prb"]
        for e in SEM_KEYS:
            sems[e] = es.enter_context(nc.semaphore(f"sm_{e}"))
        P.barrier.add("d_cst")
        P.barrier.add("d_stg")
        P.barrier.add("d_prb")
        block = es.enter_context(nc.Block())

        R = {}

        def res(name):
            if name not in R:
                R[name] = Res(name)
            return R[name]

        PR = [res(f"psum{i}") for i in range(8)]
        E = P.emit

        def dma(eng, dst, src, sem, reads=(), writes=()):
            return E(eng, lambda q, d=dst, s_=src: q.dma_start(out=d, in_=s_),
                     reads=reads, writes=writes, dma=True, sem=sem)

        for dstn, srcn in [
            ("c_idf", "id_f32"), ("c_idb", "id_bf"), ("c_mA", "mask_A"),
            ("c_tri", "tri32"), ("c_tri4", "tri32x4"), ("c_trif", "trif"),
            ("c_sel", "selpad"), ("c_or", "ones_row"),
            ("c_dcr", "dft_cur_r"), ("c_dci", "dft_cur_i"),
            ("c_dpr", "dft_prev_r"), ("c_dpi", "dft_prev_i"),
            ("c_ir", "idft_r"), ("c_ii", "idft_i"), ("c_er", "er"), ("c_ei", "ei"),
            ("w_b1", "b1_c"), ("w_b2", "b2_p"),
            ("w_cwq", "cw_q"), ("w_cwk", "cw_k"), ("w_cwv", "cw_v"),
            ("w_fs", "fir_s"), ("w_nw", "nw_t"),
        ]:
            dma("sync", sb[dstn][:], dr[srcn], "d_cst", writes=[res(dstn)])
        dma("sync", sb["lnc"][0:12, 0:1], dr["lt12"], "d_cst", writes=[res("lnc")])

        E("vector", lambda v: v.memset(sb["epsA"][:], 1e-5), writes=[res("epsA")])
        E("vector", lambda v: v.memset(sb["epsB"][:], 1e-6), writes=[res("epsB")])
        E("vector", lambda v: v.memset(sb["epsC"][:], 1e-8), writes=[res("epsC")])
        for c8 in range(8):
            E("vector", lambda v, c8=c8: v.memset(sb["v_fm"][:, c8, :, 0:32], 0.0),
              writes=[res("v_fm")])
        E("vector", lambda v: v.memset(
            sb["phal"][:].rearrange("p a b c d -> p (a b c d)"), 0.0),
          writes=[res("phal")])
        E("vector", lambda v: v.memset(sb["v_tl"][:], 0.0), writes=[res("v_tl")])
        E("vector", lambda v: v.memset(sb["S32"][:].rearrange("p a b -> p (a b)"), 0.0),
          writes=[res("S32")])
        E("vector", lambda v: v.memset(sb["S16"][:].rearrange("p a b -> p (a b)"), 0.0),
          writes=[res("S16")])
        E("vector", lambda v: v.memset(sb["slab"][:], 0.0), writes=[res("slab")])
        E("vector", lambda v: v.memset(sb["slab2"][:], 0.0), writes=[res("slab2")])
        E("vector", lambda v: v.memset(sb["U_pk"][:].rearrange("p a b -> p (a b)"), 0.0),
          writes=[res("U_pk"), res("upk0"), res("upk1"), res("upk2"), res("upk3")])
        E("vector", lambda v: v.memset(sb["ybuf"][:].rearrange("p a b -> p (a b)"), 0.0),
          writes=[res("ybuf")])

        E("scalar", lambda s_: s_.activation(sb["taurec"][:], sb["lnc"][0:12, 0:1],
                                             AF.Exp, scale=-1.0),
          reads=[res("lnc")], writes=[res("taurec")])

        dma("sync", sb["sc_a"][:, 0:256].rearrange("p (c f) -> p c f", c=16),
            dr["w2_t"].rearrange("c p f -> p c f"), "d_stg", writes=[res("sc_a")])
        E("vector", lambda v: v.tensor_copy(
            sb["w_w2"][:].rearrange("p c f -> p (c f)"), sb["sc_a"][:, 0:256]),
          reads=[res("sc_a")], writes=[res("w_w2")])
        dma("sync", sb["xn_tm"][:, 0:256].rearrange("p (c f) -> p c f", c=8),
            dr["wb_t"].rearrange("c p f -> p c f"), "d_stg", writes=[res("xn_tm")])
        E("vector", lambda v: v.tensor_copy(
            sb["w_wb"][:].rearrange("p c f -> p (c f)"), sb["xn_tm"][:, 0:256]),
          reads=[res("xn_tm")], writes=[res("w_wb")])

        fl_stage = sb["outb"][:].rearrange("p a b -> p (a b)")
        dma("sync", fl_stage[0:32, :], dr["firl_T"], "d_stg",
            writes=[res("outb0"), res("outb1")])
        E("vector", lambda v, fl_stage=fl_stage: v.tensor_copy(
            sb["d_tm"][0:32, 1, :], fl_stage[0:32, :]),
          reads=[res("outb0")], writes=[res("d_tm")])
        for half in range(2):
            cs = slice(512 * half, 512 * half + 512)
            E("tensor", lambda t, cs=cs: t.matmul(
                pb[0][0:81, :], sb["c_er"][:], sb["d_tm"][0:32, 1, cs],
                start=True, stop=True, skip_group_check=True),
              reads=[res("d_tm"), res("c_er")], writes=[PR[0]])
            E("tensor", lambda t, cs=cs: t.matmul(
                pb[1][0:81, :], sb["c_ei"][:], sb["d_tm"][0:32, 1, cs],
                start=True, stop=True, skip_group_check=True),
              reads=[res("d_tm"), res("c_ei")], writes=[PR[1]])
            E("vector", lambda v, cs=cs: v.tensor_copy(sb["whr"][0:81, cs], pb[0][0:81, :]),
              reads=[PR[0]], writes=[res("whr")])
            E("vector", lambda v, cs=cs: v.tensor_copy(sb["whi"][0:81, cs], pb[1][0:81, :]),
              reads=[PR[1]], writes=[res("whi")])

        E("vector", lambda v: v.memset(sb["wcol"][:], 0.0), writes=[res("wcol")])
        for i in range(16):
            slot = i % 2
            dma("sync",
                sb["xbuf"][:, slot, 0:512].rearrange("p (a f) -> p a f", a=2),
                dr["wft_t"][i].rearrange("p (a f) -> p a f", a=2),
                f"d_xb{slot}", writes=[res(f"xbuf{slot}")])
            E("vector", lambda v, sl=slot: v.tensor_reduce(
                sb["wcol"][:, 1:2], sb["xbuf"][:, sl, 0:512], AX.X, OP.add,
                apply_absolute_value=True),
              reads=[res(f"xbuf{slot}")], writes=[res("wcol")])
            E("vector", lambda v: v.tensor_tensor(
                sb["wcol"][:, 0:1], sb["wcol"][:, 0:1], sb["wcol"][:, 1:2], OP.add),
              reads=[res("wcol")], writes=[res("wcol")])
        E("vector", lambda v: v.memset(sb["lnc"][:, 15:16], 1.0),
          reads=[res("taurec")], writes=[res("lnc")])
        E("tensor", lambda t: t.matmul(pb[7][0:1, 0:2], sb["lnc"][:, 15:16],
                                       sb["wcol"][:, 0:2], start=True, stop=True,
                                       skip_group_check=True),
          reads=[res("wcol"), res("lnc")], writes=[PR[7]])
        E("vector", lambda v: v.tensor_scalar(sb["wcol"][0:1, 0:1], pb[7][0:1, 0:1],
                                              1.0 / (1024.0 * 1024.0), 1e-8,
                                              OP.mult, OP.add),
          reads=[PR[7]], writes=[res("wcol")])
        E("tensor", lambda t: t.matmul(pb[7][0:128, 2:3], sb["c_or"][:],
                                       sb["wcol"][0:1, 0:1], start=True, stop=True,
                                       skip_group_check=True),
          reads=[res("wcol"), res("c_or")], writes=[PR[7]])
        E("vector", lambda v: v.tensor_scalar(sb["ws127"][:], pb[7][0:128, 2:3],
                                              1.0 / 127.0, None, OP.mult),
          reads=[PR[7]], writes=[res("ws127")])
        E("vector", lambda v: v.reciprocal(sb["rws"][:], pb[7][0:128, 2:3]),
          reads=[PR[7]], writes=[res("rws")])

        def cast_stream(src, dst, n, b, op=None, hres=None):
            for i in range(n):
                slot = i % 2
                stg = sb["d_tm"][:, 0, :] if slot == 0 else sb["aq_tb"]
                stg_res = res("d_tm" if slot == 0 else "aq_tb")
                dma("sync",
                    sb["xbuf"][:, slot, :].rearrange("p (b f) -> p b f", b=b),
                    src[i], f"d_xb{slot}", writes=[res(f"xbuf{slot}")])
                if op is None:
                    E("vector", lambda v, sl=slot, stg=stg: v.tensor_copy(
                        stg[:], sb["xbuf"][:, sl, :]),
                      reads=[res(f"xbuf{slot}")], writes=[stg_res])
                else:
                    op(i, slot, stg, stg_res)
                dma("gpsimd", dst[i],
                    stg[:].rearrange("p (b f) -> p b f", b=b),
                    f"d_pre{slot}", reads=[stg_res],
                    writes=[res(f"cdone{slot}")] + ([hres] if hres is not None else []))

        for srcn, dstn in (("wq_t", "wq_b"), ("wk_t", "wk_b"), ("wv_t", "wv_b")):
            cast_stream(dr[srcn].rearrange("(a b) p f -> a p b f", b=8),
                        dr[dstn].rearrange("(a b) p f -> a p b f", b=8), 8, 8,
                        hres=res("hbm_" + dstn))
        cast_stream(dr["w1_t"].rearrange("(a b) p f -> a p b f", b=2),
                    dr["w1_b"].rearrange("(a b) p f -> a p b f", b=2), 64, 2,
                    hres=res("hbm_w1_b"))

        def wo_op(i, slot, stg, stg_res):
            for tix in range(2):
                kc = (i * 2 + tix) % 8

                def f(v, sl=slot, k=kc, tix=tix, stg=stg):
                    return v.tensor_scalar(stg[:, 512 * tix:512 * tix + 512],
                                           sb["xbuf"][:, sl, 512 * tix:512 * tix + 512],
                                           sb["w_nw"][:, k:k + 1], None, OP.mult)
                E("vector", f, reads=[res(f"xbuf{slot}"), res("w_nw")], writes=[stg_res])
        cast_stream(dr["wo_t"].rearrange("(a b) p f -> a p b f", b=2),
                    dr["wo_b"].rearrange("(a b) p f -> a p b f", b=2), 8, 2, op=wo_op,
                    hres=res("hbm_wo_b"))

        def wft_op(i, slot, stg, stg_res):
            E("vector", lambda v, sl=slot: v.tensor_scalar(
                sb["sc_a"][:], sb["xbuf"][:, sl, :], sb["rws"][:, 0:1], 1.0,
                OP.mult, OP.min),
              reads=[res(f"xbuf{slot}"), res("rws")], writes=[res("sc_a")])
            E("vector", lambda v: v.tensor_scalar(
                sb["sc_a"][:], sb["sc_a"][:], -1.0, None, OP.max),
              reads=[res("sc_a")], writes=[res("sc_a")])
            E("vector", lambda v: v.tensor_scalar(
                sb["aq8"][:], sb["sc_a"][:], 0.0, None, OP.add),
              reads=[res("sc_a")], writes=[res("aq8")])
            dbw = sb["outb"][:].rearrange("p a b -> p (a b)")
            E("vector", lambda v, dbw=dbw: v.scalar_tensor_tensor(
                dbw, sb["aq8"][:], -1.0, sb["sc_a"][:], OP.mult, OP.add),
              reads=[res("aq8"), res("sc_a")], writes=[res("outb0"), res("outb1")])
            E("vector", lambda v, stg=stg, dbw=dbw: v.scalar_tensor_tensor(
                stg[:], dbw, 0.5, sb["aq8"][:], OP.is_gt, OP.add),
              reads=[res("outb0"), res("aq8")], writes=[stg_res])
            E("vector", lambda v, stg=stg, dbw=dbw: v.scalar_tensor_tensor(
                stg[:], dbw, -0.5, stg[:], OP.is_ge, OP.add),
              reads=[res("outb0")], writes=[stg_res])
            E("vector", lambda v, stg=stg: v.tensor_scalar(
                stg[:], stg[:], -1.0, None, OP.add),
              reads=[], writes=[stg_res])
        cast_stream(dr["wft_t"].rearrange("(a b) p f -> a p b f", b=2),
                    dr["wft_b"].rearrange("(a b) p f -> a p b f", b=2), 8, 2, op=wft_op,
                    hres=res("hbm_wft_b"))

        wq_b8 = dr["wq_b"].rearrange("(a b) p f -> a p b f", b=8)
        wk_b8 = dr["wk_b"].rearrange("(a b) p f -> a p b f", b=8)
        wv_b8 = dr["wv_b"].rearrange("(a b) p f -> a p b f", b=8)
        w1_b4 = dr["w1_b"].rearrange("(a b) p f -> a p b f", b=4)
        wo_b4 = dr["wo_b"].rearrange("(a b) p f -> a p b f", b=4)
        wf_b4 = dr["wft_b"].rearrange("(a b) p f -> a p b f", b=4)

        xfm, afm = sb["xn_fm"][:], sb["aq_fm"][:]
        qfm, kfm, vfm = sb["q_fm"][:], sb["k_fm"][:], sb["v_fm"][:]
        dfm = sb["v_fm"][:][:, :, :, 32:160]
        stt = sb["stats"]

        ctr = {"w": 0, "w1": 0, "wo": 0}

        wsrcs = (wq_b8, wk_b8, wv_b8)
        wsrc_nm = ("wq_b", "wk_b", "wv_b")

        w1_b8 = dr["w1_b"].rearrange("(a b) p f -> a p b f", b=2)

        for g in range(NG):
            t0 = g * GT

            grp_w_base = ctr["w"]

            def emit_rgw(i):
                wslot = (grp_w_base + i) % 2
                wi, mc = i // 8, i % 8
                dma("sync",
                    sb["rg_w"][:, wslot, :].rearrange("p (b f) -> p b f", b=8),
                    wsrcs[wi][mc], f"d_rgw{wslot}",
                    reads=[res("hbm_" + wsrc_nm[wi])],
                    writes=[res(f"rgw{wslot}")])

            dma("sync", sb["xbuf"][:, 0, :], x_h[0, t0:t0 + GT, :],
                "d_xb0", writes=[res("xbuf0")])
            dma("sync", sb["xbuf"][:, 1, :], x_h[1, t0:t0 + GT, :],
                "d_xb1", writes=[res("xbuf1")])
            for pf in range(2):
                emit_rgw(pf)
            for b in (0, 1):
                slot = ctr["wo"] % 2
                ctr["wo"] += 1
                dma("sync",
                    sb["rg_wo"][:, slot, :].rearrange("p (b f) -> p b f", b=4),
                    wo_b4[b], f"d_rgo{slot}",
                    reads=[res("hbm_wo_b")], writes=[res(f"rgo{slot}")])

            for s in range(NSEQ):
                xslot = s % 2
                if s >= 2:
                    dma("sync", sb["xbuf"][:, xslot, :], x_h[s, t0:t0 + GT, :],
                        f"d_xb{xslot}", writes=[res(f"xbuf{xslot}")])
                xb = sb["xbuf"][:, xslot, :]
                ln = sb["lnc"]
                E("vector", lambda v, xb=xb: v.tensor_reduce(ln[:, 0:1], xb, AX.X, OP.add),
                  reads=[res(f"xbuf{xslot}")], writes=[res("lnc")])
                E("vector", lambda v, xb=xb: v.scalar_tensor_tensor(
                    sb["xn_tm"][:], xb, 1.0, xb, OP.mult, OP.mult, accum_out=ln[:, 1:2]),
                  reads=[res(f"xbuf{xslot}")], writes=[res("lnc"), res("xn_tm")])
                E("vector", lambda v: v.tensor_scalar(ln[:, 2:3], ln[:, 0:1],
                                                      1.0 / D, None, OP.mult),
                  reads=[res("lnc")], writes=[res("lnc")])
                E("vector", lambda v: v.tensor_scalar(ln[:, 3:4], ln[:, 2:3],
                                                      ln[:, 2:3], None, OP.mult),
                  reads=[res("lnc")], writes=[res("lnc")])
                E("vector", lambda v: v.scalar_tensor_tensor(
                    ln[:, 4:5], ln[:, 1:2], 1.0 / D, ln[:, 3:4], OP.mult, OP.subtract),
                  reads=[res("lnc")], writes=[res("lnc")])
                E("scalar", lambda s_: s_.activation(ln[:, 5:6], ln[:, 4:5], AF.Sqrt,
                                                     bias=sb["epsA"][:, 0:1]),
                  reads=[res("lnc"), res("epsA")], writes=[res("lnc")])
                E("vector", lambda v: v.reciprocal(ln[:, 6:7], ln[:, 5:6]),
                  reads=[res("lnc")], writes=[res("lnc")])
                E("vector", lambda v, xb=xb: v.tensor_scalar(
                    sb["xn_tm"][:], xb, ln[:, 2:3], ln[:, 6:7], OP.subtract, OP.mult),
                  reads=[res(f"xbuf{xslot}"), res("lnc")], writes=[res("xn_tm")])
                E("vector", lambda v: v.reduce_max(ln[:, 7:8], sb["xn_tm"][:], axis=AX.X,
                                                   apply_absolute_value=True),
                  reads=[res("xn_tm")], writes=[res("lnc")])
                E("vector", lambda v: v.tensor_tensor(ln[:, 8:9], ln[:, 7:8],
                                                      sb["epsC"][:, 0:1], OP.add),
                  reads=[res("lnc"), res("epsC")], writes=[res("lnc")])
                E("vector", lambda v: v.reciprocal(ln[:, 9:10], ln[:, 8:9]),
                  reads=[res("lnc")], writes=[res("lnc")])
                E("vector", lambda v: v.tensor_scalar(ln[:, 10:11], ln[:, 9:10],
                                                      127.0, None, OP.mult),
                  reads=[res("lnc")], writes=[res("lnc")])
                E("vector", lambda v, s=s: v.tensor_scalar(
                    sb["sffn"][:, s:s + 1], ln[:, 8:9], sb["ws127"][:, 0:1], None, OP.mult),
                  reads=[res("lnc"), res("ws127")], writes=[res("sffn")])
                E("vector", lambda v: v.tensor_scalar(
                    sb["sc_a"][:], sb["xn_tm"][:], ln[:, 10:11], 127.0, OP.mult, OP.min),
                  reads=[res("xn_tm"), res("lnc")], writes=[res("sc_a")])
                E("vector", lambda v: v.tensor_scalar(
                    sb["sc_a"][:], sb["sc_a"][:], -128.0, None, OP.max),
                  reads=[res("sc_a")], writes=[res("sc_a")])
                E("vector", lambda v: v.tensor_scalar(
                    sb["aq8"][:], sb["sc_a"][:], 0.0, None, OP.add),
                  reads=[res("sc_a")], writes=[res("aq8")])
                dbuf = sb["xbuf"][:, xslot, :]
                E("vector", lambda v, dbuf=dbuf: v.scalar_tensor_tensor(
                    dbuf, sb["aq8"][:], -1.0, sb["sc_a"][:], OP.mult, OP.add),
                  reads=[res("aq8"), res("sc_a")], writes=[res(f"xbuf{xslot}")])
                E("vector", lambda v, dbuf=dbuf: v.scalar_tensor_tensor(
                    sb["aq_tb"][:], dbuf, 0.5, sb["aq8"][:],
                    OP.is_gt, OP.add),
                  reads=[res(f"xbuf{xslot}"), res("aq8")], writes=[res("aq_tb")])
                E("vector", lambda v, dbuf=dbuf: v.scalar_tensor_tensor(
                    sb["aq_tb"][:], dbuf, -0.5, sb["aq_tb"][:],
                    OP.is_ge, OP.add),
                  reads=[res(f"xbuf{xslot}"), res("aq_tb")], writes=[res("aq_tb")])
                E("vector", lambda v: v.tensor_scalar(
                    sb["aq_tb"][:], sb["aq_tb"][:], -1.0, None, OP.add),
                  reads=[res("aq_tb")], writes=[res("aq_tb")])
                xn16 = sb["q2b"][:].rearrange("p a b -> p (a b)")
                E("vector", lambda v, xn16=xn16: v.tensor_copy(xn16, sb["xn_tm"][:]),
                  reads=[res("xn_tm")], writes=[res("q2b")])
                for c8 in range(8):
                    E("tensor", lambda t, c8=c8, xn16=xn16: t.transpose(
                        pb[2][:, 128 * c8:128 * c8 + 128],
                        xn16[:, 128 * c8:128 * c8 + 128], sb["c_idb"][:]),
                      reads=[res("q2b"), res("c_idb")], writes=[PR[2]])
                E("vector", lambda v, s=s: v.tensor_copy(
                    xfm[:, :, s, :], pb[2][:].rearrange("p (c t) -> p c t", c=8)),
                  reads=[PR[2]], writes=[res("xn_fm")])
                for c8 in range(8):
                    E("tensor", lambda t, c8=c8: t.transpose(
                        pb[3][:, 128 * c8:128 * c8 + 128],
                        sb["aq_tb"][:, 128 * c8:128 * c8 + 128], sb["c_idb"][:]),
                      reads=[res("aq_tb"), res("c_idb")], writes=[PR[3]])
                E("vector", lambda v, s=s: v.tensor_copy(
                    afm[:, :, s, :], pb[3][:].rearrange("p (c t) -> p c t", c=8)),
                  reads=[PR[3]], writes=[res("aq_fm")])

            first_stats = [True]

            def stats_mm(lhsT, rhs, reads):
                st_ = first_stats[0]
                first_stats[0] = False
                E("tensor", lambda t, l=lhsT, r=rhs, st=st_: t.matmul(
                    pb[7][0:32, :], l, r, start=st, stop=False, skip_group_check=True),
                  reads=reads, writes=[PR[7]])

            for kc in range(8):
                stats_mm(sb["w_wb"][:, kc, :],
                         xfm[:, kc, :, :].rearrange("p s t -> p (s t)"),
                         [res("w_wb"), res("xn_fm")])

            conv_w = {"q": sb["w_cwq"], "k": sb["w_cwk"], "v": sb["w_cwv"]}
            dst_fm = {"q": qfm, "k": kfm, "v": vfm}
            for wi, wname in enumerate(("q", "k", "v")):
                for mc in range(8):
                    li = wi * 8 + mc
                    wslot = (grp_w_base + li) % 2
                    ctr["w"] += 1
                    bk = mc % 2
                    for kc in range(8):
                        E("tensor", lambda t, sl=wslot, kc=kc, bk=bk: t.matmul(
                            pb[bk][:], sb["rg_w"][:, sl, 128 * kc:128 * kc + 128],
                            xfm[:, kc, :, :].rearrange("p s t -> p (s t)"),
                            start=(kc == 0), stop=(kc == 7), skip_group_check=True),
                          reads=[res(f"rgw{wslot}"), res("xn_fm")], writes=[PR[bk]])
                    if li + 2 < 24:
                        emit_rgw(li + 2)
                    pslot = mc % 2
                    pbf = sb["prebuf"][:, pslot, :, :]
                    pb_res = res(f"prebuf{pslot}")
                    E("vector", lambda v, wi=wi, mc=mc, pbf=pbf: v.tensor_copy(
                        pbf[:, :, 0:4], sb["phal"][:, wi, mc, :, :]),
                      reads=[res("phal")], writes=[pb_res])
                    E("scalar", lambda s_, bk=bk, pbf=pbf: s_.copy(
                        pbf[:, :, 4:132], pb[bk][:].rearrange("p (s t) -> p s t", s=NSEQ)),
                      reads=[PR[bk]], writes=[pb_res])
                    E("vector", lambda v, wi=wi, mc=mc, pbf=pbf: v.tensor_copy(
                        sb["phal"][:, wi, mc, :, :], pbf[:, :, 128:132]),
                      reads=[pb_res], writes=[res("phal")])
                    ca = sb["cacc"][:, pslot, :, :]
                    ca_res = res(f"cacc{pslot}")
                    cw = conv_w[wname]
                    E("vector", lambda v, pbf=pbf, ca=ca, cw=cw, mc=mc: v.tensor_scalar(
                        ca, pbf[:, :, 4:132], cw[:, mc, 3:4], None, OP.mult),
                      reads=[pb_res], writes=[ca_res])
                    for j in range(3):
                        sh = 3 - j
                        E("vector", lambda v, pbf=pbf, ca=ca, cw=cw, mc=mc, j=j, sh=sh:
                          v.scalar_tensor_tensor(
                              ca, pbf[:, :, 4 - sh:132 - sh], cw[:, mc, j:j + 1],
                              ca, OP.mult, OP.add),
                          reads=[pb_res, ca_res], writes=[ca_res])
                    if wname == "v":
                        dst = vfm[:, mc, :, 32:160]
                    else:
                        dst = dst_fm[wname][:, mc, :, :]
                    E("scalar", lambda s_, ca=ca, dst=dst: s_.activation(dst, ca, AF.Silu),
                      reads=[ca_res], writes=[res(wname + "_fm")])
                    if wname in ("q", "k"):
                        qs = mc % 2
                        srcv = dst_fm[wname][:, mc, :, :].rearrange("p s t -> p (s t)")
                        E("vector", lambda v, srcv=srcv, qs=qs: v.tensor_tensor(
                            sb["q2b"][:, qs, :], srcv, srcv, OP.mult),
                          reads=[res(wname + "_fm")], writes=[res(f"q2b{qs}")])
                        row = 4 * wi + mc // 2
                        stats_mm(sb["c_sel"][:, 32 - row:64 - row], sb["q2b"][:, qs, :],
                                 [res(f"q2b{qs}"), res("c_sel")])

            if PROBE and g == 0:
                E("gpsimd", lambda q: q.dma_start(out=dr["pr_xnfm"], in_=xfm),
                  reads=[res("xn_fm")], dma=True, sem="d_prb")
                E("gpsimd", lambda q: q.dma_start(out=dr["pr_qfm"], in_=qfm),
                  reads=[res("q_fm")], dma=True, sem="d_prb")
                E("gpsimd", lambda q: q.dma_start(out=dr["pr_kfm"], in_=kfm),
                  reads=[res("k_fm")], dma=True, sem="d_prb")
                E("gpsimd", lambda q: q.dma_start(out=dr["pr_vfm"], in_=vfm),
                  reads=[res("v_fm")], dma=True, sem="d_prb")

            E("scalar", lambda s_: s_.activation(sb["slab"][0:32, :], pb[7][0:32, :],
                                                 AF.Sigmoid),
              reads=[PR[7]], writes=[res("slab")])
            E("scalar", lambda s_: s_.activation(sb["slab2"][0:8, :], pb[7][0:8, :],
                                                 AF.Sqrt, bias=sb["epsB"][0:8, 0:1]),
              reads=[PR[7], res("epsB")], writes=[res("slab2")])
            E("vector", lambda v: v.reciprocal(sb["slab"][0:8, :], sb["slab2"][0:8, :]),
              reads=[res("slab2")], writes=[res("slab")])
            for s in range(NSEQ):
                E("tensor", lambda t, s=s: t.transpose(
                    pb[6][:, 0:32], sb["slab"][0:32, 128 * s:128 * s + 128],
                    sb["c_idf"][0:32, 0:32]),
                  reads=[res("slab"), res("c_idf")], writes=[PR[6]])
                E("vector", lambda v, s=s: v.tensor_copy(stt[:, s, 0:12], pb[6][:, 0:12]),
                  reads=[PR[6]], writes=[res("stats")])
                E("vector", lambda v, s=s: v.tensor_tensor(
                    stt[:, s, 12:16], stt[:, s, 8:12], stt[:, s, 4:8], OP.mult),
                  reads=[res("stats")], writes=[res("stats")])
                E("vector", lambda v, s=s: v.tensor_tensor(
                    stt[:, s, 16:20], stt[:, s, 12:16], stt[:, s, 4:8], OP.mult),
                  reads=[res("stats")], writes=[res("stats")])
                E("vector", lambda v, s=s: v.tensor_scalar(
                    stt[:, s, 16:20], stt[:, s, 16:20], -1.0, None, OP.mult),
                  reads=[res("stats")], writes=[res("stats")])

            for s in range(NSEQ):
                for bi_, (srcn, dstn) in enumerate((("k_fm", "k_tm"), ("v_fm", "v_tm"))):
                    bk = 2 + bi_
                    for c8 in range(8):
                        if dstn == "v_tm":
                            inp = vfm[:, c8, s, 32:160]
                        else:
                            inp = kfm[:, c8, s, :]
                        E("tensor", lambda t, inp=inp, c8=c8, bk=bk: t.transpose(
                            pb[bk][:, 128 * c8:128 * c8 + 128], inp, sb["c_idb"][:]),
                          reads=[res(srcn), res("c_idb")], writes=[PR[bk]])
                    E("vector", lambda v, s=s, dstn=dstn, bk=bk: v.tensor_copy(
                        sb[dstn][:, s, :], pb[bk][:]),
                      reads=[PR[bk]], writes=[res(dstn)])

            def dft_emit(gi):
                u, seg = gi // 3, gi % 3
                s_, chalf = u // 2, u % 2
                cs = slice(512 * chalf, 512 * chalf + 512)
                if seg == 0:
                    E("tensor", lambda t, s_=s_, cs=cs: t.matmul(
                        pb[7][0:81, :], sb["c_dcr"][:], sb["v_tm"][:, s_, cs],
                        start=True, stop=False, skip_group_check=True),
                      reads=[res("v_tm"), res("c_dcr")], writes=[PR[7]])
                    E("tensor", lambda t, s_=s_, cs=cs: t.matmul(
                        pb[7][0:81, :], sb["c_dpr"][32 * s_:32 * s_ + 32, :],
                        sb["v_tl"][32 * s_:32 * s_ + 32, cs],
                        start=False, stop=True, skip_group_check=True,
                        tile_position=(32 * s_, 0)),
                      reads=[res("v_tl"), res("c_dpr")], writes=[PR[7]])
                    E("scalar", lambda sc: sc.copy(sb["xh"][0:81, 0, :], pb[7][0:81, :]),
                      reads=[PR[7]], writes=[res("xh0")])
                elif seg == 1:
                    E("tensor", lambda t, s_=s_, cs=cs: t.matmul(
                        pb[7][0:81, :], sb["c_dci"][:], sb["v_tm"][:, s_, cs],
                        start=True, stop=False, skip_group_check=True),
                      reads=[res("v_tm"), res("c_dci")], writes=[PR[7]])
                    E("tensor", lambda t, s_=s_, cs=cs: t.matmul(
                        pb[7][0:81, :], sb["c_dpi"][32 * s_:32 * s_ + 32, :],
                        sb["v_tl"][32 * s_:32 * s_ + 32, cs],
                        start=False, stop=True, skip_group_check=True,
                        tile_position=(32 * s_, 0)),
                      reads=[res("v_tl"), res("c_dpi")], writes=[PR[7]])
                    E("scalar", lambda sc: sc.copy(sb["xh"][0:81, 1, :], pb[7][0:81, :]),
                      reads=[PR[7]], writes=[res("xh1")])
                    E("vector", lambda v, cs=cs: v.tensor_tensor(
                        sb["yh"][0:81, 0, :], sb["xh"][0:81, 0, :], sb["whr"][0:81, cs],
                        OP.mult),
                      reads=[res("xh0"), res("whr")], writes=[res("yh0")])
                    E("vector", lambda v, cs=cs: v.tensor_tensor(
                        sb["yh"][0:81, 2, :], sb["xh"][0:81, 1, :], sb["whi"][0:81, cs],
                        OP.mult),
                      reads=[res("xh1"), res("whi")], writes=[res("yh2")])
                    E("vector", lambda v: v.tensor_tensor(
                        sb["yh"][0:81, 0, :], sb["yh"][0:81, 0, :], sb["yh"][0:81, 2, :],
                        OP.subtract),
                      reads=[res("yh0"), res("yh2")], writes=[res("yh0")])
                    E("vector", lambda v, cs=cs: v.tensor_tensor(
                        sb["yh"][0:81, 1, :], sb["xh"][0:81, 0, :], sb["whi"][0:81, cs],
                        OP.mult),
                      reads=[res("xh0"), res("whi")], writes=[res("yh1")])
                    E("vector", lambda v, cs=cs: v.tensor_tensor(
                        sb["yh"][0:81, 3, :], sb["xh"][0:81, 1, :], sb["whr"][0:81, cs],
                        OP.mult),
                      reads=[res("xh1"), res("whr")], writes=[res("yh3")])
                    E("vector", lambda v: v.tensor_tensor(
                        sb["yh"][0:81, 1, :], sb["yh"][0:81, 1, :], sb["yh"][0:81, 3, :],
                        OP.add),
                      reads=[res("yh1"), res("yh3")], writes=[res("yh1")])
                else:
                    E("tensor", lambda t: t.matmul(
                        pb[7][:, :], sb["c_ir"][:], sb["yh"][0:81, 0, :],
                        start=True, stop=False, skip_group_check=True),
                      reads=[res("yh0"), res("c_ir")], writes=[PR[7]])
                    E("tensor", lambda t: t.matmul(
                        pb[7][:, :], sb["c_ii"][:], sb["yh"][0:81, 1, :],
                        start=False, stop=True, skip_group_check=True),
                      reads=[res("yh1"), res("c_ii")], writes=[PR[7]])
                    E("scalar", lambda sc, s_=s_, cs=cs: sc.copy(
                        sb["fl_tm"][:, s_, cs], pb[7][:]),
                      reads=[PR[7]], writes=[res("fl_tm")])

            grp_w1_base = ctr["w1"]

            def emit_rg1(i):
                slot = (grp_w1_base + i) % 4
                dma("sync",
                    sb["rg_w1"][:, slot, :].rearrange("p (b f) -> p b f", b=2),
                    w1_b8[i], f"d_rg1{slot}",
                    reads=[res("hbm_w1_b")], writes=[res(f"rg1{slot}")])

            for pf in range(3):
                emit_rg1(pf)

            for h in range(H):
                kcs = (2 * h, 2 * h + 1)
                pxof = [0, 1, 4, 5]
                for s in range(NSEQ):
                    bk = pxof[s]
                    for ki, kc in enumerate(kcs):
                        E("tensor", lambda t, kc=kc, s=s, ki=ki, bk=bk: t.matmul(
                            pb[bk][:, 0:128], kfm[:, kc, s, :], kfm[:, kc, s, :],
                            start=(ki == 0), stop=False, skip_group_check=True),
                          reads=[res("k_fm")], writes=[PR[bk]])
                        E("tensor", lambda t, kc=kc, s=s, ki=ki, bk=bk: t.matmul(
                            pb[bk][:, 128:256], kfm[:, kc, s, :], qfm[:, kc, s, :],
                            start=False, stop=(ki == 1), skip_group_check=True),
                          reads=[res("k_fm"), res("q_fm")], writes=[PR[bk]])
                    E("vector", lambda v, s=s, bk=bk: v.tensor_copy(
                        sb["G_sb"][:, s, :], pb[bk][:, 0:256]),
                      reads=[PR[bk]], writes=[res("G_sb")])
                    E("vector", lambda v, s=s, h=h, bk=bk: v.scalar_tensor_tensor(
                        sb["A_sb"][:, s, :], pb[bk][:, 0:128],
                        stt[:, s, 16 + h:17 + h], sb["c_mA"][:], OP.mult, OP.mult),
                      reads=[PR[bk], res("stats"), res("c_mA")], writes=[res("A_sb")])
                    E("tensor", lambda t, s=s: t.transpose(
                        pb[3][:, 0:128], sb["A_sb"][:, s, :], sb["c_idb"][:]),
                      reads=[res("A_sb"), res("c_idb")], writes=[PR[3]])
                    E("vector", lambda v, s=s: v.tensor_copy(
                        sb["B_sb"][:, s, :], pb[3][:, 0:128]),
                      reads=[PR[3]], writes=[res("B_sb")])
                    E("vector", lambda v, s=s: v.tensor_tensor(
                        sb["P_sb"][:, s, :], sb["B_sb"][:, s, :], sb["c_idb"][:], OP.add),
                      reads=[res("B_sb"), res("c_idb")], writes=[res("P_sb")])
                if PROBE and g == 0 and h == 0:
                    E("gpsimd", lambda q: q.dma_start(out=dr["pr_A"], in_=sb["A_sb"][:]),
                      reads=[res("A_sb")], dma=True, sem="d_prb")
                    E("gpsimd", lambda q: q.dma_start(out=dr["pr_B"], in_=sb["B_sb"][:]),
                      reads=[res("B_sb")], dma=True, sem="d_prb")
                    E("gpsimd", lambda q: q.dma_start(out=dr["pr_P1"], in_=sb["P_sb"][:]),
                      reads=[res("P_sb")], dma=True, sem="d_prb")
                for lev in range(6):
                    for s in range(NSEQ):
                        E("tensor", lambda t, s=s: t.matmul(
                            pb[6][:, 128 * s:128 * s + 128],
                            sb["B_sb"][:, s, :], sb["A_sb"][:, s, :],
                            start=(s == 0), stop=(s == 3), skip_group_check=True),
                          reads=[res("A_sb"), res("B_sb")],
                          writes=[PR[6]] + ([res("pu0"), res("pu1")]
                                            if s == 0 else []))
                    E("vector", lambda v: v.tensor_copy(
                        sb["A_sb"][:].rearrange("p a b -> p (a b)"), pb[6][:]),
                      reads=[PR[6]], writes=[res("A_sb")])
                    for s in range(NSEQ):
                        E("tensor", lambda t, s=s: t.transpose(
                            pb[3][:, 128 * s:128 * s + 128],
                            sb["A_sb"][:, s, :], sb["c_idb"][:]),
                          reads=[res("A_sb"), res("c_idb")], writes=[PR[3]])
                    E("vector", lambda v: v.tensor_copy(
                        sb["B_sb"][:].rearrange("p a b -> p (a b)"),
                        pb[3][:, 0:512]),
                      reads=[PR[3]], writes=[res("B_sb")])
                    for s in range(NSEQ):
                        E("tensor", lambda t, s=s: t.matmul(
                            pb[6][:, 128 * s:128 * s + 128],
                            sb["A_sb"][:, s, :], sb["P_sb"][:, s, :],
                            start=(s == 0), stop=False, skip_group_check=True),
                          reads=[res("A_sb"), res("P_sb")], writes=[PR[6]])
                        E("tensor", lambda t, s=s: t.matmul(
                            pb[6][:, 128 * s:128 * s + 128],
                            sb["c_idb"][:], sb["P_sb"][:, s, :],
                            start=False, stop=(s == 3), skip_group_check=True),
                          reads=[res("c_idb"), res("P_sb")], writes=[PR[6]])
                    E("vector", lambda v: v.tensor_copy(
                        sb["P_sb"][:].rearrange("p a b -> p (a b)"), pb[6][:]),
                      reads=[PR[6]], writes=[res("P_sb")])
                    dft_emit(h * 6 + lev)
                    if PROBE and g == 0 and h == 0 and lev < 4:
                        E("gpsimd", lambda q, lev=lev: q.dma_start(
                            out=dr["pr_A2"][lev], in_=sb["A_sb"][:]),
                          reads=[res("A_sb")], dma=True, sem="d_prb")
                        E("gpsimd", lambda q, lev=lev: q.dma_start(
                            out=dr["pr_Pl"][lev], in_=sb["P_sb"][:]),
                          reads=[res("P_sb")], dma=True, sem="d_prb")
                for s in range(NSEQ):
                    bk = pxof[s]
                    E("vector", lambda v, s=s, h=h: v.tensor_scalar(
                        sb["vb_sb"][:, s, :], sb["v_tm"][:, s, 256 * h:256 * h + 256],
                        stt[:, s, 12 + h:13 + h], None, OP.mult),
                      reads=[res("v_tm"), res("stats")], writes=[res("vb_sb")])
                    for ki, kc in enumerate(kcs):
                        E("tensor", lambda t, kc=kc, s=s, ki=ki, h=h, bk=bk: t.matmul(
                            pb[bk][:, 0:256], kfm[:, kc, s, :],
                            sb["S16"][:, s * 4 + h, 256 * ki:256 * ki + 256],
                            start=(ki == 0), stop=False, skip_group_check=True),
                          reads=[res("k_fm"), res("S16")], writes=[PR[bk]])
                        E("tensor", lambda t, kc=kc, s=s, ki=ki, h=h, bk=bk: t.matmul(
                            pb[bk][:, 256:512], qfm[:, kc, s, :],
                            sb["S16"][:, s * 4 + h, 256 * ki:256 * ki + 256],
                            start=False, stop=False, skip_group_check=True),
                          reads=[res("q_fm"), res("S16")], writes=[PR[bk]])
                for s in range(NSEQ):
                    bk = pxof[s]
                    ub, uc = ((6, 0), (6, 256), (7, 0), (7, 256))[s]
                    E("vector", lambda v, s=s, h=h, bk=bk: v.scalar_tensor_tensor(
                        sb["ybuf"][:, s, :], pb[bk][:, 0:256],
                        stt[:, s, 16 + h:17 + h], sb["vb_sb"][:, s, :],
                        OP.mult, OP.add),
                      reads=[PR[bk], res("stats"), res("vb_sb")], writes=[res("ybuf")])
                    E("tensor", lambda t, s=s, ub=ub, uc=uc: t.matmul(
                        pb[ub][:, uc:uc + 256], sb["P_sb"][:, s, :],
                        sb["ybuf"][:, s, :],
                        start=True, stop=True, skip_group_check=True),
                      reads=[res("P_sb"), res("ybuf")],
                      writes=[res(f"pu{s}")] + ([PR[7]] if ub == 7 else []))
                    E("scalar", lambda sc, s=s, ub=ub, uc=uc: sc.copy(
                        sb["U_pk"][:, s, :], pb[ub][:, uc:uc + 256]),
                      reads=[res(f"pu{s}")], writes=[res(f"upk{s}")])
                    E("vector", lambda v, s=s: v.tensor_tensor(
                        sb["A_sb"][:, s, :], sb["G_sb"][:, s, 128:256],
                        sb["c_trif"][:], OP.mult),
                      reads=[res("G_sb"), res("c_trif")], writes=[res("A_sb")])
                    E("tensor", lambda t, s=s, bk=bk: t.matmul(
                        pb[bk][:, 256:512], sb["A_sb"][:, s, :],
                        sb["U_pk"][:, s, :],
                        start=False, stop=False, skip_group_check=True),
                      reads=[res("A_sb"), res(f"upk{s}")], writes=[PR[bk]])
                if PROBE and g == 0 and h == 0:
                    E("gpsimd", lambda q: q.dma_start(out=dr["pr_G"], in_=sb["G_sb"][:]),
                      reads=[res("G_sb")], dma=True, sem="d_prb")
                    E("gpsimd", lambda q: q.dma_start(out=dr["pr_P"], in_=sb["P_sb"][:]),
                      reads=[res("P_sb")], dma=True, sem="d_prb")
                    E("gpsimd", lambda q: q.dma_start(out=dr["pr_U"], in_=sb["U_pk"][:]),
                      reads=[res("U_pk")], dma=True, sem="d_prb")
                for s in range(NSEQ):
                    bk = pxof[s]
                    E("vector", lambda v, s=s, h=h, bk=bk: v.tensor_scalar(
                        sb["d_tm"][:, s, 256 * h:256 * h + 256], pb[bk][:, 256:512],
                        stt[:, s, h:h + 1], None, OP.mult),
                      reads=[PR[bk], res("stats")], writes=[res("d_tm")])
                    for ki in range(2):
                        E("tensor", lambda t, s=s, h=h, ki=ki: t.matmul(
                            pb[7][:, 256 * ki:256 * ki + 256],
                            sb["k_tm"][:, s, 256 * h + 128 * ki:256 * h + 128 * ki + 128],
                            sb["U_pk"][:, s, :],
                            start=(ki == 0), stop=(ki == 1), skip_group_check=True),
                          reads=[res("k_tm"), res(f"upk{s}")],
                          writes=[PR[7]] + ([res("pu2"), res("pu3")]
                                            if (s == 0 and ki == 0) else []))
                    E("vector", lambda v, s=s, h=h: v.tensor_tensor(
                        sb["S32"][:, s * 4 + h, :], sb["S32"][:, s * 4 + h, :],
                        pb[7][:], OP.add),
                      reads=[PR[7], res("S32")], writes=[res("S32")])
                    E("scalar", lambda v, s=s, h=h: v.copy(
                        sb["S16"][:, s * 4 + h, :], sb["S32"][:, s * 4 + h, :]),
                      reads=[res("S32")], writes=[res("S16")])

            for mc in range(8):
                cslot = mc % 2
                ca = sb["cacc"][:, cslot, :, :]
                ca_res = res(f"cacc{cslot}")
                vv = vfm[:, mc, :, :]
                E("vector", lambda v, vv=vv, ca=ca, mc=mc: v.tensor_scalar(
                    ca, vv[:, :, 32:160], sb["w_fs"][:, mc, 2:3], None, OP.mult),
                  reads=[res("v_fm"), res("w_fs")], writes=[ca_res])
                for j in range(2):
                    sh = 2 - j
                    E("vector", lambda v, vv=vv, ca=ca, mc=mc, j=j, sh=sh:
                      v.scalar_tensor_tensor(
                          ca, vv[:, :, 32 - sh:160 - sh], sb["w_fs"][:, mc, j:j + 1],
                          ca, OP.mult, OP.add),
                      reads=[res("v_fm"), ca_res, res("w_fs")], writes=[ca_res])
                E("vector", lambda v, ca=ca, mc=mc: v.tensor_copy(
                    sb["fs_fm"][:, mc, :, :], ca),
                  reads=[ca_res], writes=[res("q_fm")])
            for c8 in range(8):
                E("vector", lambda v, c8=c8: v.tensor_copy(
                    vfm[:, c8, :, 0:32], vfm[:, c8, :, 128:160]),
                  reads=[res("v_fm")], writes=[res("v_fm")])

            for s in range(NSEQ):
                E("gpsimd", lambda q, s=s: q.dma_start(
                    out=sb["v_tl"][32 * s:32 * s + 30, :], in_=sb["v_tm"][98:128, s, :]),
                  reads=[res("v_tm")], writes=[res("v_tl")], dma=True, sem="d_vtl")

            if PROBE and g == 0:
                E("gpsimd", lambda q: q.dma_start(out=dr["pr_dtm"], in_=sb["d_tm"][:]),
                  reads=[res("d_tm")], dma=True, sem="d_prb")

            if PROBE and g == 0:
                E("gpsimd", lambda q: q.dma_start(out=dr["pr_flfm"], in_=sb["fl_fm"][:]),
                  reads=[res("k_fm")], dma=True, sem="d_prb")
                E("gpsimd", lambda q: q.dma_start(out=dr["pr_fsfm"], in_=sb["fs_fm"][:]),
                  reads=[res("q_fm")], dma=True, sem="d_prb")

            for s in range(NSEQ):
                bk = 2 + s % 2
                for c8 in range(8):
                    E("tensor", lambda t, s=s, c8=c8, bk=bk: t.transpose(
                        pb[bk][:, 128 * c8:128 * c8 + 128],
                        sb["d_tm"][:, s, 128 * c8:128 * c8 + 128], sb["c_idb"][:]),
                      reads=[res("d_tm"), res("c_idb")], writes=[PR[bk]])
                E("vector", lambda v, s=s, bk=bk: v.tensor_copy(
                    dfm[:, :, s, :], pb[bk][:].rearrange("p (c t) -> p c t", c=8)),
                  reads=[PR[bk]], writes=[res("v_fm")])
            for s in range(NSEQ):
                bk = 2 + (s + 1) % 2
                for c8 in range(8):
                    E("tensor", lambda t, s=s, c8=c8, bk=bk: t.transpose(
                        pb[bk][:, 128 * c8:128 * c8 + 128],
                        sb["fl_tm"][:, s, 128 * c8:128 * c8 + 128], sb["c_idb"][:]),
                      reads=[res("fl_tm"), res("c_idb")], writes=[PR[bk]])
                E("vector", lambda v, s=s, bk=bk: v.tensor_copy(
                    sb["fl_fm"][:, :, s, :], pb[bk][:].rearrange("p (c t) -> p c t", c=8)),
                  reads=[PR[bk]], writes=[res("k_fm")])

            gate_res = [res("xn_fm"), res("q_fm"), res("k_fm"), res("v_fm")]
            gate_srcs = [xfm, sb["fs_fm"][:], sb["fl_fm"][:], dfm]
            first_f2 = [True]
            for quarter in range(4):
                for kc in range(32):
                    if kc % 2 == 0:
                        li = quarter * 16 + kc // 2
                        ctr["w1"] += 1
                        if li + 3 < 64:
                            emit_rg1(li + 3)
                    slot = (grp_w1_base + li) % 4
                    src = gate_srcs[kc // 8]
                    kcc = kc % 8
                    for mc4 in range(4):
                        fbk = (0, 1, 4, 5)[mc4]
                        if kc // 8 == 3:
                            for s4 in range(NSEQ):
                                E("tensor", lambda t, kc=kc, kcc=kcc, mc4=mc4, slot=slot,
                                  src=src, fbk=fbk, s4=s4:
                                  t.matmul(
                                      pb[fbk][:, 128 * s4:128 * s4 + 128],
                                      sb["rg_w1"][:, slot,
                                                  512 * (kc % 2) + 128 * mc4:
                                                  512 * (kc % 2) + 128 * mc4 + 128],
                                      src[:, kcc, s4, :],
                                      start=False, stop=(kc == 31 and s4 == 3),
                                      skip_group_check=True),
                                  reads=[res(f"rg1{slot}"), gate_res[kc // 8]],
                                  writes=[PR[fbk]])
                        else:
                            E("tensor", lambda t, kc=kc, kcc=kcc, mc4=mc4, slot=slot,
                              src=src, fbk=fbk:
                              t.matmul(
                                  pb[fbk][:],
                                  sb["rg_w1"][:, slot,
                                              512 * (kc % 2) + 128 * mc4:
                                              512 * (kc % 2) + 128 * mc4 + 128],
                                  src[:, kcc, :, :].rearrange("p s t -> p (s t)"),
                                  start=(kc == 0), stop=False, skip_group_check=True),
                              reads=[res(f"rg1{slot}"), gate_res[kc // 8]],
                              writes=[PR[fbk]])
                for mc4 in range(4):
                    fbk = (0, 1, 4, 5)[mc4]
                    mc = quarter * 4 + mc4
                    hs = mc % 2
                    E("scalar", lambda s_, mc=mc, fbk=fbk, hs=hs: s_.activation(
                        sb["hidr"][:, hs, :], pb[fbk][:], AF.Gelu,
                        bias=sb["w_b1"][:, mc:mc + 1]),
                      reads=[PR[fbk], res("w_b1")], writes=[res(f"hid{hs}")])
                    stf = first_f2[0]
                    first_f2[0] = False
                    E("tensor", lambda t, mc=mc, hs=hs, stf=stf: t.matmul(
                        pb[7][0:16, :], sb["w_w2"][:, mc, :], sb["hidr"][:, hs, :],
                        start=stf, stop=(mc == 15), skip_group_check=True),
                      reads=[res(f"hid{hs}"), res("w_w2")], writes=[PR[7]])

            E("vector", lambda v: v.tensor_scalar(
                sb["slab"][0:16, :], pb[7][0:16, :], sb["w_b2"][0:16, 0:1], None, OP.add),
              reads=[PR[7], res("w_b2")], writes=[res("slab")])
            E("vector", lambda v: v.tensor_scalar(
                sb["slab"][0:12, :], sb["slab"][0:12, :], sb["taurec"][0:12, 0:1],
                None, OP.mult),
              reads=[res("slab"), res("taurec")], writes=[res("slab")])
            for s in range(NSEQ):
                E("tensor", lambda t, s=s: t.transpose(
                    pb[6][:, 0:32], sb["slab"][0:32, 128 * s:128 * s + 128],
                    sb["c_idf"][0:32, 0:32]),
                  reads=[res("slab"), res("c_idf")], writes=[PR[6]])
                E("vector", lambda v, s=s: v.tensor_copy(stt[:, s, 20:36], pb[6][:, 0:16]),
                  reads=[PR[6]], writes=[res("stats")])
                E("scalar", lambda s_, s=s: s_.activation(
                    stt[:, s, 36:40], stt[:, s, 32:36], AF.Sigmoid),
                  reads=[res("stats")], writes=[res("stats")])
                E("scalar", lambda s_, s=s: s_.activation(
                    stt[:, s, 20:32], stt[:, s, 20:32], AF.Exp),
                  reads=[res("stats")], writes=[res("stats")])
                E("vector", lambda v, s=s: v.tensor_tensor(
                    sb["gt4"][:, 0:4], stt[:, s, 20:24], stt[:, s, 24:28], OP.add),
                  reads=[res("stats")], writes=[res("gt4")])
                E("vector", lambda v, s=s: v.tensor_tensor(
                    sb["gt4"][:, 0:4], sb["gt4"][:, 0:4], stt[:, s, 28:32], OP.add),
                  reads=[res("gt4"), res("stats")], writes=[res("gt4")])
                E("vector", lambda v: v.reciprocal(sb["gt4"][:, 0:4], sb["gt4"][:, 0:4]),
                  reads=[res("gt4")], writes=[res("gt4")])
                E("vector", lambda v, s=s: v.tensor_scalar(
                    sb["gt4"][:, 4:8], stt[:, s, 36:40], 0.09, 0.9, OP.mult, OP.add),
                  reads=[res("stats")], writes=[res("gt4")])
                E("vector", lambda v, s=s: v.tensor_tensor(
                    stt[:, s, 36:40], sb["gt4"][:, 4:8], stt[:, s, 36:40], OP.mult),
                  reads=[res("gt4"), res("stats")], writes=[res("stats")])
                E("vector", lambda v, s=s: v.tensor_scalar(
                    sb["gt4"][:, 4:8], stt[:, s, 36:40], -1.0, 1.0, OP.mult, OP.add),
                  reads=[res("stats")], writes=[res("gt4")])
                E("vector", lambda v: v.tensor_tensor(
                    sb["gt4"][:, 4:8], sb["gt4"][:, 4:8], sb["gt4"][:, 0:4], OP.mult),
                  reads=[res("gt4")], writes=[res("gt4")])
                for gix in range(3):
                    E("vector", lambda v, s=s, gix=gix: v.tensor_tensor(
                        stt[:, s, 20 + 4 * gix:24 + 4 * gix],
                        stt[:, s, 20 + 4 * gix:24 + 4 * gix], sb["gt4"][:, 4:8], OP.mult),
                      reads=[res("stats"), res("gt4")], writes=[res("stats")])

            if PROBE and g == 0:
                E("gpsimd", lambda q: q.dma_start(out=dr["pr_stats"], in_=stt[:]),
                  reads=[res("stats")], dma=True, sem="d_prb")
                E("gpsimd", lambda q: q.dma_start(out=dr["pr_slab"], in_=sb["slab"][:]),
                  reads=[res("slab")], dma=True, sem="d_prb")

            for s in range(NSEQ):
                bk = 2 + s % 2
                for c8 in range(8):
                    E("tensor", lambda t, s=s, c8=c8, bk=bk: t.transpose(
                        pb[bk][:, 128 * c8:128 * c8 + 128],
                        sb["fs_fm"][:, c8, s, :], sb["c_idb"][:]),
                      reads=[res("q_fm"), res("c_idb")], writes=[PR[bk]])
                E("vector", lambda v, bk=bk: v.tensor_copy(
                    sb["fstm"][:].rearrange("p a b -> p (a b)"), pb[bk][:]),
                  reads=[PR[bk]], writes=[res("ybuf")])
                for h in range(4):
                    hsl = slice(256 * h, 256 * h + 256)
                    E("vector", lambda v, s=s, h=h, hsl=hsl: v.tensor_scalar(
                        sb["o_tm"][:].rearrange("p a b -> p (a b)")[:, hsl], sb["v_tm"][:, s, hsl],
                        stt[:, s, 36 + h:37 + h], None, OP.mult),
                      reads=[res("v_tm"), res("stats")], writes=[res("U_pk"), res("upk0"), res("upk1"), res("upk2"), res("upk3")])
                    E("vector", lambda v, s=s, h=h, hsl=hsl: v.scalar_tensor_tensor(
                        sb["o_tm"][:].rearrange("p a b -> p (a b)")[:, hsl], sb["d_tm"][:, s, hsl],
                        stt[:, s, 28 + h:29 + h], sb["o_tm"][:].rearrange("p a b -> p (a b)")[:, hsl], OP.mult, OP.add),
                      reads=[res("d_tm"), res("stats"), res("U_pk")], writes=[res("U_pk")])
                    E("vector", lambda v, s=s, h=h, hsl=hsl: v.scalar_tensor_tensor(
                        sb["o_tm"][:].rearrange("p a b -> p (a b)")[:, hsl], sb["fl_tm"][:, s, hsl],
                        stt[:, s, 24 + h:25 + h], sb["o_tm"][:].rearrange("p a b -> p (a b)")[:, hsl], OP.mult, OP.add),
                      reads=[res("fl_tm"), res("stats"), res("U_pk")], writes=[res("U_pk")])
                    E("vector", lambda v, s=s, h=h, hsl=hsl: v.scalar_tensor_tensor(
                        sb["o_tm"][:].rearrange("p a b -> p (a b)")[:, hsl], sb["fstm"][:].rearrange("p a b -> p (a b)")[:, hsl],
                        stt[:, s, 20 + h:21 + h], sb["o_tm"][:].rearrange("p a b -> p (a b)")[:, hsl], OP.mult, OP.add),
                      reads=[res("ybuf"), res("stats"), res("U_pk")], writes=[res("U_pk")])
                    E("vector", lambda v, h=h, hsl=hsl: v.scalar_tensor_tensor(
                        sb["xn_tm"][:, hsl], sb["o_tm"][:].rearrange("p a b -> p (a b)")[:, hsl], 1.0, sb["o_tm"][:].rearrange("p a b -> p (a b)")[:, hsl],
                        OP.mult, OP.mult, accum_out=sb["o2c"][:, h:h + 1]),
                      reads=[res("U_pk")], writes=[res("o2c"), res("xn_tm")])
                if PROBE and g == 0:
                    E("gpsimd", lambda q, s=s: q.dma_start(
                        out=dr["pr_otm"][s],
                        in_=sb["o_tm"][:].rearrange("p a b -> p (a b)")),
                      reads=[res("U_pk")], dma=True, sem="d_prb")
                E("scalar", lambda s_: s_.activation(sb["o2c"][:, 0:4], sb["o2c"][:, 0:4],
                                                     AF.Sqrt, bias=sb["epsA"][:, 0:1],
                                                     scale=1.0 / 256.0),
                  reads=[res("o2c"), res("epsA")], writes=[res("o2c")])
                E("vector", lambda v: v.reciprocal(sb["o2c"][:, 0:4], sb["o2c"][:, 0:4]),
                  reads=[res("o2c")], writes=[res("o2c")])
                for h in range(4):
                    hsl = slice(256 * h, 256 * h + 256)
                    E("vector", lambda v, h=h, hsl=hsl: v.tensor_scalar(
                        sb["orms"][:, hsl], sb["o_tm"][:].rearrange("p a b -> p (a b)")[:, hsl], sb["o2c"][:, h:h + 1],
                        None, OP.mult),
                      reads=[res("U_pk"), res("o2c")], writes=[res("aq_tb")])
                if PROBE and g == 0:
                    E("gpsimd", lambda q, s=s: q.dma_start(
                        out=dr["pr_orms"][s], in_=sb["orms"][:]),
                      reads=[res("aq_tb")], dma=True, sem="d_prb")
                bko = 2 + (s + 1) % 2
                for c8 in range(8):
                    E("tensor", lambda t, c8=c8, bko=bko: t.transpose(
                        pb[bko][:, 128 * c8:128 * c8 + 128],
                        sb["orms"][:, 128 * c8:128 * c8 + 128], sb["c_idb"][:]),
                      reads=[res("aq_tb"), res("c_idb")], writes=[PR[bko]])
                E("vector", lambda v, s=s, bko=bko: v.tensor_copy(
                    sb["d_tm"][:, s, :], pb[bko][:]),
                  reads=[PR[bko]], writes=[res("d_tm")])

            for half in range(2):
                if half == 1:
                    for b in (2 * half, 2 * half + 1):
                        slot = ctr["wo"] % 2
                        ctr["wo"] += 1
                        dma("sync",
                            sb["rg_wo"][:, slot, :].rearrange("p (b f) -> p b f", b=4),
                            wo_b4[b], f"d_rgo{slot}",
                            reads=[res("hbm_wo_b")], writes=[res(f"rgo{slot}")])
                base = ctr["wo"] - 2
                for s in range(NSEQ):
                    for kc in range(8):
                        bi = half * 8 + kc
                        slot = (base + (bi % 8) // 4) % 2
                        kbk = (0, 1, 4, 5)[s]
                        E("tensor", lambda t, s=s, kc=kc, bi=bi, slot=slot, kbk=kbk:
                          t.matmul(
                            pb[kbk][:], sb["d_tm"][:, s, 128 * kc:128 * kc + 128],
                            sb["rg_wo"][:, slot, 512 * (bi % 4):512 * (bi % 4) + 512],
                            start=(kc == 0), stop=(kc == 7), skip_group_check=True),
                          reads=[res("d_tm"), res(f"rgo{slot}")], writes=[PR[kbk]])
                for s in range(NSEQ):
                    xslot = s % 2
                    tbuf = sb["sc_a"] if s < 2 else sb["xn_tm"]
                    tr_res = res("sc_a" if s < 2 else "xn_tm")
                    dma("sync", sb["xbuf"][:, xslot, :], x_h[s, t0:t0 + GT, :],
                        f"d_xb{xslot}", writes=[res(f"xbuf{xslot}")])
                    kbk = (0, 1, 4, 5)[s]
                    E("vector", lambda v, s=s, half=half, xslot=xslot, tbuf=tbuf, kbk=kbk:
                      v.tensor_tensor(
                        tbuf[:, 512 * (s % 2):512 * (s % 2) + 512], pb[kbk][:],
                        sb["xbuf"][:, xslot, 512 * half:512 * half + 512], OP.add),
                      reads=[PR[kbk], res(f"xbuf{xslot}")], writes=[tr_res])
                for b in (2 * half, 2 * half + 1):
                    slot = ctr["wo"] % 2
                    ctr["wo"] += 1
                    dma("sync",
                        sb["rg_wo"][:, slot, :].rearrange("p (b f) -> p b f", b=4),
                        wf_b4[b], f"d_rgo{slot}",
                        reads=[res("hbm_wft_b")], writes=[res(f"rgo{slot}")])
                base = ctr["wo"] - 2
                for s in range(NSEQ):
                    for kc in range(8):
                        bi = half * 8 + kc
                        slot = (base + (bi % 8) // 4) % 2
                        kbk = (0, 1, 4, 5)[s]
                        E("tensor", lambda t, s=s, kc=kc, bi=bi, slot=slot, kbk=kbk:
                          t.matmul(
                            pb[kbk][:], afm[:, kc, s, :],
                            sb["rg_wo"][:, slot, 512 * (bi % 4):512 * (bi % 4) + 512],
                            start=(kc == 0), stop=(kc == 7), skip_group_check=True),
                          reads=[res("aq_fm"), res(f"rgo{slot}")], writes=[PR[kbk]])
                for s in range(NSEQ):
                    oslot = s % 2
                    tbuf = sb["sc_a"] if s < 2 else sb["xn_tm"]
                    tr_res = res("sc_a" if s < 2 else "xn_tm")
                    kbk = (0, 1, 4, 5)[s]
                    E("vector", lambda v, s=s, half=half, oslot=oslot, tbuf=tbuf, kbk=kbk:
                      v.scalar_tensor_tensor(
                          sb["outb"][:, oslot, :],
                          pb[kbk][:], sb["sffn"][:, s:s + 1],
                          tbuf[:, 512 * (s % 2):512 * (s % 2) + 512], OP.mult, OP.add),
                      reads=[PR[kbk], res("sffn"), tr_res],
                      writes=[res(f"outb{oslot}")])
                    dma("gpsimd", out_h[s, t0:t0 + GT, 512 * half:512 * half + 512],
                        sb["outb"][:, oslot, :], f"d_out{oslot}",
                        reads=[res(f"outb{oslot}")], writes=[res(f"outd{oslot}")])

        cum = P.finalize()

        @block.sync
        def _(q):
            P.replay("sync", q, sems, cum)

        @block.tensor
        def _(t):
            P.replay("tensor", t, sems, cum)

        @block.vector
        def _(v):
            P.replay("vector", v, sems, cum)

        @block.scalar
        def _(s_):
            P.replay("scalar", s_, sems, cum)

        @block.gpsimd
        def _(q):
            P.replay("gpsimd", q, sems, cum)
            for k in sorted(P.tot):
                if k.startswith("d_") and k in sems:
                    q.wait_ge(sems[k], P.tot[k])

    return nc


_CACHED = {}


def kernel(**inputs):
    x = np.ascontiguousarray(np.asarray(inputs["L0_attn_hidden_states"], np.float32))
    B, L, _ = x.shape
    nper = B // NCORES

    w = _prep_weights(inputs)
    c = _host_consts()

    key = ("main", L)
    if key not in _CACHED:
        _CACHED[key] = build_program(L=L)
    nc = _CACHED[key]

    base = {}
    base.update({k: np.ascontiguousarray(v) for k, v in w.items()})
    base.update({k: np.ascontiguousarray(v) for k, v in c.items()})

    in_maps = []
    for ci in range(NCORES):
        m = dict(base)
        m["x"] = np.ascontiguousarray(x[ci * nper:(ci + 1) * nper])
        in_maps.append(m)

    res_ = run_bass_kernel_spmd(nc, in_maps, core_ids=list(range(NCORES)))
    out = np.concatenate([r["out"] for r in res_.results], axis=0)
    return out
